# version 19 (final)
# baseline (speedup 1.0000x reference)
import numpy as np
import ml_dtypes
import concourse.bass as bass
import concourse.mybir as mybir
from concourse.bass_utils import run_bass_kernel_spmd

F32 = mybir.dt.float32
BF16 = mybir.dt.bfloat16
AF = mybir.ActivationFunctionType
ALU = mybir.AluOpType

E = 1024
S = 2048
CH = 8
TC = 4
TT = 16
NH = 16
NE = 32
EI = 512

_TAPS = []
_N_PHASES = 99


class Prog:

    ENGS = ("t", "v", "s", "g", "y")
    CHANNELS = ("sm", "x", "wm", "wm2", "w0", "w1", "w2", "vb", "gup", "dwn", "y", "y2")

    def __init__(self, nc):
        self.nc = nc
        self.ops = {e: [] for e in self.ENGS}
        self.cnt = {e: 0 for e in self.ENGS}
        for ch in self.CHANNELS:
            self.cnt["g:" + ch] = 0
        self.lastwait = {}
        self.sems = {}

    def op(self, eng, f, inc=None):
        if inc is None:
            inc = 1 if eng in ("v", "s") else 0
        if inc:
            self.cnt[eng] += inc

            def g(E, f=f, name=eng, n=inc, self=self):
                f(E).then_inc(self.sems[name], n)

            self.ops[eng].append(g)
        else:
            self.ops[eng].append(f)
        return self.cnt[eng]

    def wait(self, eng, src, val):
        if val is None or val <= 0:
            return
        key = (eng, src)
        if self.lastwait.get(key, 0) >= val:
            return
        self.lastwait[key] = val

        def g(E, src=src, v=val, self=self):
            E.wait_ge(self.sems[src], v)

        self.ops[eng].append(g)

    def tick(self, eng):
        return self.cnt[eng]

    def load(self, ch, dst, src, eng="g"):
        key = "g:" + ch
        self.cnt[key] += 16

        def f(E, d=dst, s=src, key=key, self=self):
            E.dma_start(out=d, in_=s).then_inc(self.sems[key], 16)

        self.ops[eng].append(f)
        return self.cnt[key]

    def dr(self, eng):
        self.wait(eng, eng, self.cnt[eng])


def build_program(n_phases=99, taps=()):
    nc = bass.Bass("TRN2", target_bir_lowering=False)

    d_xT = nc.dram_tensor("xT", [E, S], F32, kind="ExternalInput")
    d_cT = nc.dram_tensor("cT", [128, 8], F32, kind="ExternalInput")
    d_wmod = nc.dram_tensor("wmod", [E, 6 * E], F32, kind="ExternalInput")
    d_qkvT = nc.dram_tensor("qkvT", [E, 3 * E], F32, kind="ExternalInput")
    d_qkvN = nc.dram_tensor("qkvN", [2 * E, E], F32, kind="ExternalInput")
    d_rot = nc.dram_tensor("rot", [E, E], F32, kind="ExternalInput")
    d_ent = nc.dram_tensor("ent", [E, E], F32, kind="ExternalInput")
    d_outwT = nc.dram_tensor("outwT", [E, E], F32, kind="ExternalInput")
    d_qkv_bT = nc.dram_tensor("qkv_bT", [128, 24], F32, kind="ExternalInput")
    d_vb_row = nc.dram_tensor("vb_row", [1, E], F32, kind="ExternalInput")
    d_out_bT = nc.dram_tensor("out_bT", [128, 8], F32, kind="ExternalInput")
    d_sgwT = nc.dram_tensor("sgwT", [E, E], F32, kind="ExternalInput")
    d_suwT = nc.dram_tensor("suwT", [E, E], F32, kind="ExternalInput")
    d_sdwT = nc.dram_tensor("sdwT", [E, E], F32, kind="ExternalInput")
    d_rwT = nc.dram_tensor("rwT", [E, NE], F32, kind="ExternalInput")
    d_esb = nc.dram_tensor("esb_row", [1, NE], F32, kind="ExternalInput")
    d_gupT = nc.dram_tensor("gupT", [NE, E, E], F32, kind="ExternalInput")
    d_downT = nc.dram_tensor("downT", [NE, EI, E], F32, kind="ExternalInput")
    d_out = nc.dram_tensor("outT", [E, S], F32, kind="ExternalOutput")

    d_eye = nc.inline_tensor(np.eye(128, dtype=ml_dtypes.bfloat16), "eye_inl")

    AR = 208128
    arena = nc.alloc_sbuf_tensor("arena", [128, AR], mybir.dt.uint8)
    base = nc.lookup_mloc(arena).addr

    def at(name, shape, dtype, off):
        return nc.alloc_sbuf_tensor_at(name, shape, dtype, offset=base + off)

    R_X, R_H, R_Q, R_K, R_V, R_W = 0, 32768, 65536, 98304, 131072, 165120
    x_sb = at("x_sb", [128, CH, S], BF16, R_X)
    h_sb = at("h_sb", [128, CH, S], BF16, R_H)
    q_sb = at("q_sb", [128, CH, S], BF16, R_Q)
    y_sb = at("y_sb", [128, TT, E], BF16, R_Q)
    k_sb = at("k_sb", [128, CH, S], BF16, R_K)
    qstage_sb = at("qstage_sb", [128, 2, CH, 512], BF16, R_K)
    t1a2_sb = at("t1a2_sb", [128, 512], F32, R_K + 16384)
    t1b2_sb = at("t1b2_sb", [128, 512], F32, R_K + 18432)
    hsh_sb = at("hsh_sb", [128, CH, S], BF16, R_K)
    hmid_sb = at("hmid_sb", [128, 2, 4, EI], BF16, R_V)
    sgt_sb = at("sgt_sb", [128, 2, 512], BF16, R_V + 8192)
    lbuf_sb = at("lbuf_sb", [128, TT, NE], F32, R_V + 10240)
    rw_sb = at("rw_sb", [128, TT, NE], F32, R_V + 12288)
    comb_sb = at("comb_sb", [128, TT, NE], F32, R_V + 14336)
    v_sb = at("v_sb", [128, TT, NH * 65], BF16, R_V)
    mw_sb = at("mw_sb", [128, CH, E], BF16, R_V)
    kstage_sb = at("kstage_sb", [128, 2, CH, 512], BF16, R_V)
    ostage_sb = at("ostage_sb", [128, 2, S], F32, R_V + 16384)

    W0, W1, W2 = R_W, R_W + 16384, R_W + 32768
    wms_sb = at("wms_sb", [128, 2, CH, 512], F32, W0)
    qkvq_sb = at("qkvq_sb", [128, CH, E], BF16, W0)
    rot_sb = at("rot_sb", [128, CH, E], BF16, W1)
    qkvk_sb = at("qkvk_sb", [128, CH, E], BF16, W0)
    ent_sb = at("ent_sb", [128, CH, E], BF16, W1)
    qkvv_sb = at("qkvv_sb", [128, CH, E], BF16, W0)
    outw_sb = at("outw_sb", [128, CH, E], BF16, W1)
    pt_sb = at("pt_sb", [128, 4, 1024], BF16, W0)
    att_sb = at("att_sb", [128, CH, 512], BF16, W0 + 8192)
    t1a_sb = at("t1a_sb", [128, 512], F32, W0 + 8192)
    t1b_sb = at("t1b_sb", [128, 512], F32, W0 + 12288)
    sgw_sb = at("sgw_sb", [128, CH, E], BF16, W0)
    suw_sb = at("suw_sb", [128, CH, E], BF16, W1)
    sdw_sb = at("sdw_sb", [128, CH, E], BF16, W1)
    gup_sb = at("gup_sb", [128, CH, E], BF16, W0)
    down_sb = at("down_sb", [128, 4, E], BF16, W1)
    r1_sb = at("r1_sb", [1, 512], F32, W2)
    r2_sb = at("r2_sb", [1, 512], F32, W2 + 2048)
    r3_sb = at("r3_sb", [1, 512], F32, W2 + 4096)
    rbf1_sb = at("rbf1_sb", [1, 512], BF16, W2 + 6144)
    rbf2_sb = at("rbf2_sb", [1, 512], BF16, W2 + 7168)
    rc_sb = at("rc_sb", [1, 1024], F32, W2)
    rcbf_sb = at("rcbf_sb", [1, 1024], BF16, W2 + 6144)
    sq_sb = at("sq_sb", [128, 2, 512], BF16, W2 + 8192)
    vbias_sb = at("vbias_sb", [128, E], BF16, W2)
    vbrow2_sb = at("vbrow2_sb", [1, E], BF16, W2 + 2048)
    bcn2_sb = at("bcn2_sb", [128, 2, 512], BF16, W2 + 8192)

    eye_sb = nc.alloc_sbuf_tensor("eye_sb", [128, 128], BF16)
    ones_rb = nc.alloc_sbuf_tensor("ones_rb", [1, 128], BF16)
    ones_rf = nc.alloc_sbuf_tensor("ones_rf", [1, 128], F32)
    ones_cb = nc.alloc_sbuf_tensor("ones_cb", [128, 1], BF16)
    one1f_sb = nc.alloc_sbuf_tensor("one1f_sb", [1, 1], F32)
    kc_sb = nc.alloc_sbuf_tensor("kc_sb", [128, 3], F32)
    ct_sb = nc.alloc_sbuf_tensor("ct_sb", [128, 8], F32)
    cs_sb = nc.alloc_sbuf_tensor("cs_sb", [128, 8], F32)
    qkvb_sb = nc.alloc_sbuf_tensor("qkvb_sb", [128, 24], F32)
    outb_sb = nc.alloc_sbuf_tensor("outb_sb", [128, 8], F32)

    esbrow_sb = nc.alloc_sbuf_tensor("esbrow_sb", [1, NE], F32)
    esb_sb = nc.alloc_sbuf_tensor("esb_sb", [128, NE], F32)
    rwt_sb = nc.alloc_sbuf_tensor("rwt_sb", [128, CH, NE], BF16)
    macc_sb = nc.alloc_sbuf_tensor("macc_sb", [128, 48], F32)
    deriv_sb = nc.alloc_sbuf_tensor("deriv_sb", [128, 16], F32)
    gg_sb = nc.alloc_sbuf_tensor("gg_sb", [128, 16], F32)
    gbo_sb = nc.alloc_sbuf_tensor("gbo_sb", [128, 8], F32)
    bqr_sb = nc.alloc_sbuf_tensor("bqr_sb", [128, 16], F32)
    qkvbb_sb = nc.alloc_sbuf_tensor("qkvbb_sb", [128, 16], BF16)
    tmp16_sb = nc.alloc_sbuf_tensor("tmp16_sb", [128, 16], F32)
    tmp16b_sb = nc.alloc_sbuf_tensor("tmp16b_sb", [128, 16], F32)
    top8_sb = nc.alloc_sbuf_tensor("top8_sb", [128, 8], F32)
    rr_sb = nc.alloc_sbuf_tensor("rr_sb", [128, 2], F32)


    pA = nc.alloc_psum_tensor("pA", [128, 2048], F32)
    pB = nc.alloc_psum_tensor("pB", [128, 2048], F32)

    P = Prog(nc)

    def slA(i, w=512):
        return pA[:, i * 512:i * 512 + w]

    def slB(i, w=512):
        return pB[:, i * 512:i * 512 + w]

    def mm_group(out_ap, terms, inc=1):
        def f(t, out=out_ap, terms=tuple(terms)):
            n = len(terms)
            mm = None
            for i, (l, r) in enumerate(terms):
                mm = t.matmul(out, l, r, start=(i == 0), stop=(i == n - 1))
            return mm
        return P.op("t", f, inc=inc)

    P.op("v", lambda v: v.memset(ones_rb[:], 1.0))
    P.op("v", lambda v: v.memset(ones_rf[:], 1.0))
    P.op("v", lambda v: v.memset(ones_cb[:], 1.0))
    P.op("v", lambda v: v.memset(kc_sb[:, 0:1], 2.0))
    P.op("v", lambda v: v.memset(kc_sb[:, 1:2], -1.0))
    P.op("v", lambda v: v.memset(kc_sb[:, 2:3], 1e-5))
    P.op("v", lambda v: v.memset(one1f_sb[:], 1.0), inc=1)
    v_ones_const = P.tick("v")

    P.load("sm", eye_sb[:], d_eye.ap())
    P.load("sm", ct_sb[:], d_cT.ap())
    P.load("sm", qkvb_sb[:], d_qkv_bT.ap())
    P.load("sm", outb_sb[:], d_out_bT.ap())
    P.load("sm", esbrow_sb[:], d_esb.ap())
    g_small = P.load("sm", rwt_sb[:], d_rwT.ap().rearrange("(c p) f -> p c f", p=128))
    g_eye = g_ct = g_esb = g_rwt = g_small
    g_x = P.load("x", x_sb[:], d_xT.ap().rearrange("(c p) t -> p c t", p=128))

    P.wait("s", "g:sm", g_ct)
    P.op("s", lambda s: s.activation(out=cs_sb[:], in_=ct_sb[:], func=AF.Silu), inc=1)
    t_silu_c = P.tick("s")

    mmg_t = {}
    tr_t = {}
    r1c_v = {}
    macc_v = {}
    for s in range(12):
        ch = "wm" if s % 2 == 0 else "wm2"
        P.wait("y", "t", mmg_t.get(s - 2, 0))
        gt = P.load(ch, wms_sb[:, s % 2],
                    d_wmod.ap()[:, s * 512:(s + 1) * 512].rearrange("(c p) f -> p c f", p=128),
                    eng="y")
        P.wait("t", "g:" + ch, gt)
        P.wait("t", "s", t_silu_c)
        P.wait("t", "v", r1c_v.get(s - 2, 0))
        rowp = pA[0:1, (s % 2) * 512:(s % 2) * 512 + 512]
        def f(t, s=s, rowp=rowp):
            mm = None
            for k in range(CH):
                mm = t.matmul(rowp, cs_sb[:, k:k + 1], wms_sb[:, s % 2, k, :],
                              start=(k == 0), stop=(k == CH - 1))
            return mm
        P.op("t", f, inc=1)
        mmg_t[s] = P.tick("t")
        P.wait("v", "t", mmg_t[s])
        P.wait("v", "t", tr_t.get(s - 1, 0))
        P.op("v", lambda v, rowp=rowp: v.tensor_copy(r1_sb[:], rowp), inc=1)
        r1c_v[s] = P.tick("v")
        P.wait("t", "v", r1c_v[s])
        P.wait("t", "v", macc_v.get(s - 2, 0))
        trp = pA[:, 1024 + (s % 2) * 512:1024 + (s % 2) * 512 + 4]
        def f2(t, trp=trp):
            mm = None
            for b in range(4):
                mm = t.transpose(trp[:, b:b + 1], r1_sb[0:1, b * 128:(b + 1) * 128],
                                 one1f_sb[:])
            return mm
        P.op("t", f2, inc=1)
        tr_t[s] = P.tick("t")
        P.wait("v", "t", tr_t[s])
        P.op("v", lambda v, s=s, trp=trp: v.tensor_copy(macc_sb[:, s * 4:s * 4 + 4], trp), inc=1)
        macc_v[s] = P.tick("v")
    tick_mod_v = P.tick("v")
    t_mod_mm = mmg_t[11]

    P.dr("v")
    P.op("v", lambda v: v.tensor_scalar_add(deriv_sb[:, 0:8], macc_sb[:, 8:16], 1.0))
    P.op("v", lambda v: v.tensor_scalar_add(deriv_sb[:, 8:16], macc_sb[:, 32:40], 1.0), inc=1)
    t_deriv0 = P.tick("v")
    P.wait("s", "v", t_deriv0)
    P.op("s", lambda s: s.activation(out=tmp16_sb[:, 0:8], in_=macc_sb[:, 16:24], func=AF.Abs))
    P.op("s", lambda s: s.activation(out=tmp16_sb[:, 8:16], in_=macc_sb[:, 40:48], func=AF.Abs), inc=1)
    t_abs = P.tick("s")
    P.wait("v", "s", t_abs)
    P.op("v", lambda v: v.tensor_scalar_max(tmp16b_sb[:], tmp16_sb[:], 0.25))
    P.op("v", lambda v: v.tensor_scalar(gg_sb[:, 0:8], macc_sb[:, 16:24], 0.0, None, op0=ALU.is_ge))
    P.op("v", lambda v: v.tensor_scalar(gg_sb[:, 8:16], macc_sb[:, 40:48], 0.0, None, op0=ALU.is_ge))
    P.dr("v")
    P.op("v", lambda v: v.scalar_tensor_tensor(
        out=tmp16_sb[:], in0=gg_sb[:], scalar=kc_sb[:, 0:1], in1=tmp16b_sb[:],
        op0=ALU.mult, op1=ALU.mult))
    P.dr("v")
    P.op("v", lambda v: v.tensor_sub(gg_sb[:], tmp16_sb[:], tmp16b_sb[:]))
    P.dr("v")
    P.wait("v", "g:sm", g_small)
    P.op("v", lambda v: v.tensor_mul(gbo_sb[:], gg_sb[:, 0:8], outb_sb[:]), inc=1)
    t_mod_done = P.tick("v")

    if n_phases <= 1:
        return _finish(nc, P, taps, locals())

    LNS = {"prev_var": 0, "h_done": {}}

    def layernorm(src_get, dst_get, onep_col, shift_col, fold_ga, tc_range,
                  pre_v_tick=None, t1pair=None, pshome=None, hook=None):
        s_last = v_last = 0
        for tc in tc_range:
            ts = slice(tc * 512, (tc + 1) * 512)
            vpre = pre_v_tick(tc) if pre_v_tick else None
            if vpre:
                P.wait("s", "v", vpre)
                P.wait("t", "v", vpre)
            P.wait("t", "v", LNS["prev_var"])
            s1 = pshome[0] if pshome else pA[0:1, 512:1024]
            s2 = pshome[1] if pshome else pA[0:1, 1024:1536]
            mm_group(s1, [(ones_cb[:], src_get(c)[:, ts]) for c in range(CH)], inc=1)
            sq_t = [0, 0]
            for c in range(CH):
                if c >= 2:
                    P.wait("v", "t", sq_t[c % 2])
                P.op("v", lambda v, c=c, ts=ts: v.tensor_mul(
                    sq_sb[:, c % 2, :], src_get(c)[:, ts], src_get(c)[:, ts]), inc=1)
                tk = P.tick("v")
                P.wait("t", "v", tk)
                P.op("t", lambda t, c=c, s2=s2: t.matmul(
                    s2, ones_cb[:], sq_sb[:, c % 2, :],
                    start=(c == 0), stop=(c == CH - 1)), inc=1)
                sq_t[c % 2] = P.tick("t")
            t_s2 = P.tick("t")
            if hook:
                hook(tc, 0)
            P.wait("v", "t", t_s2)
            P.op("v", lambda v, s1=s1: v.tensor_scalar_mul(r1_sb[:], s1, 1.0 / E))
            P.op("v", lambda v, s2=s2: v.tensor_scalar_mul(r2_sb[:], s2, 1.0 / E))
            P.dr("v")
            P.op("v", lambda v: v.tensor_mul(r3_sb[:], r1_sb[:], r1_sb[:]))
            P.dr("v")
            P.op("v", lambda v: v.tensor_sub(r2_sb[:], r2_sb[:], r3_sb[:]), inc=1)
            t_var = P.tick("v")
            LNS["prev_var"] = t_var
            P.wait("s", "v", t_var)
            P.op("s", lambda s: s.activation(out=r3_sb[:], in_=r2_sb[:], func=AF.Ln,
                                             bias=kc_sb[0:1, 2:3]), inc=1)
            P.dr("s")
            P.op("s", lambda s: s.activation(out=r3_sb[:], in_=r3_sb[:], func=AF.Exp,
                                             scale=-0.5), inc=1)
            t_rstd = P.tick("s")
            P.wait("v", "s", t_rstd)
            P.op("v", lambda v: v.scalar_tensor_tensor(
                out=rbf2_sb[:], in0=r1_sb[:], scalar=kc_sb[0:1, 1:2], in1=r3_sb[:],
                op0=ALU.mult, op1=ALU.mult))
            P.op("v", lambda v: v.tensor_copy(rbf1_sb[:], r3_sb[:]), inc=1)
            t_rows = P.tick("v")
            P.wait("t", "v", t_rows)
            rb, nb = (pshome[2], pshome[3]) if pshome else (slB(2), slB(3))
            P.op("t", lambda t, rb=rb: t.matmul(rb, ones_rb[:], rbf1_sb[:], start=True, stop=True))
            P.op("t", lambda t, nb=nb: t.matmul(nb, ones_rb[:], rbf2_sb[:], start=True, stop=True), inc=1)
            t_bc = P.tick("t")
            if hook:
                hook(tc, 1)
            h_ticks = {}
            for c in range(CH):
                P.wait("v", "t", t_bc)
                t1 = (t1pair or (t1a_sb, t1b_sb))[c % 2]
                P.op("v", lambda v, c=c, ts=ts, t1=t1, rb=rb: v.tensor_mul(
                    t1[:], src_get(c)[:, ts], rb))
                P.dr("v")
                P.op("v", lambda v, t1=t1, nb=nb: v.tensor_add(t1[:], t1[:], nb), inc=1)
                P.dr("v")
                P.op("v", lambda v, c=c, ts=ts, t1=t1: v.tensor_scalar(
                    dst_get(c)[:, ts], t1[:],
                    deriv_sb[:, onep_col + c:onep_col + c + 1],
                    macc_sb[:, shift_col + c:shift_col + c + 1],
                    op0=ALU.mult, op1=ALU.add), inc=1)
                h_ticks[c] = P.tick("v")
                if fold_ga:
                    P.dr("v")
                    P.op("v", lambda v, c=c, ts=ts: v.scalar_tensor_tensor(
                        out=x_sb[:, c, ts], in0=h_sb[:, c, ts],
                        scalar=gg_sb[:, c:c + 1], in1=x_sb[:, c, ts],
                        op0=ALU.mult, op1=ALU.add), inc=1)
                    t_xt = P.tick("v")
                    P.wait("s", "v", t_xt)
                    P.op("s", lambda s, c=c, ts=ts: s.activation(
                        out=x_sb[:, c, ts], in_=x_sb[:, c, ts], func=AF.Identity,
                        bias=gbo_sb[:, c:c + 1]), inc=1)
            LNS["h_done"][tc] = h_ticks[CH - 1]
            s_last, v_last = P.tick("s"), P.tick("v")
        return s_last, v_last

    qnat2_sb = at("qnat2_sb", [128, CH, E], BF16, R_K)
    rot2_sb = at("rot2_sb", [128, CH, E], BF16, R_Q)
    g_natq = P.load("w0", qnat2_sb[:], d_qkvN.ap()[0:E].rearrange("(c p) k -> p c k", p=128))
    g_rot = P.load("w1", rot2_sb[:], d_rot.ap().rearrange("(c p) f -> p c f", p=128))
    P.wait("g", "t", t_mod_mm)
    g_ent = P.load("w2", ent_sb[:], d_ent.ap().rearrange("(c p) f -> p c f", p=128))

    brow_sb = at("brow_sb", [1, 512], F32, R_V + 16384)
    trb_t = [0]
    trb_s = [0]

    def make_rot_emitter(g_nat, g_w, w_sb, prev_s=0, ch_w="w1"):
        st = {"gx": 0, "mw_s": {}, "last_t": 0}

        def emit(n):
            for _ in range(n):
                gx = st["gx"]
                kc, j2 = gx // 2, gx % 2
                slot = slB(gx % 2)
                P.wait("t", "g:w0", g_nat)
                P.wait("t", "g:" + ch_w, g_w)
                P.wait("t", "s", st["mw_s"].get(gx - 2, prev_s))
                mm_group(slot, [(qnat2_sb[:, ic, kc * 128:(kc + 1) * 128],
                                 w_sb[:, ic, j2 * 512:(j2 + 1) * 512])
                                for ic in range(CH)])
                tk = P.tick("t")
                st["last_t"] = tk
                P.wait("s", "t", tk)
                P.op("s", lambda s, kc=kc, j2=j2, slot=slot: s.activation(
                    out=mw_sb[:, kc, j2 * 512:(j2 + 1) * 512], in_=slot,
                    func=AF.Identity), inc=1)
                st["mw_s"][gx] = P.tick("s")
                st["gx"] = gx + 1
        return st, emit

    def emit_rot_bias(bias_cols, bq_dst_col, w_sb):
        rowp = pA[0:1, 1536:2048]
        for j2 in range(2):
            P.wait("t", "s", trb_s[0])
            P.wait("t", "v", v_qkvbb)

            def fb(t, j2=j2, rowp=rowp, bias_cols=bias_cols, w_sb=w_sb):
                mm = None
                for ic in range(CH):
                    mm = t.matmul(rowp[:, 0:512], qkv_bT[:, bias_cols + ic:bias_cols + ic + 1],
                                  w_sb[:, ic, j2 * 512:(j2 + 1) * 512],
                                  start=(ic == 0), stop=(ic == CH - 1))
                return mm
            P.op("t", fb, inc=1)
            tk = P.tick("t")
            P.wait("v", "t", tk)
            P.wait("v", "t", trb_t[0])
            P.op("v", lambda v, rowp=rowp: v.tensor_copy(brow_sb[:], rowp[:, 0:512]), inc=1)
            vr = P.tick("v")
            P.wait("t", "v", vr)
            for b in range(4):
                P.op("t", lambda t, b=b: t.transpose(
                    pA[:, 1536 + b:1537 + b], brow_sb[0:1, b * 128:(b + 1) * 128],
                    one1f_sb[:]), inc=1 if b == 3 else 0)
            trb_t[0] = P.tick("t")
            P.wait("s", "t", trb_t[0])
            P.op("s", lambda s, j2=j2, bq_dst_col=bq_dst_col: s.copy(
                bqr_sb[:, bq_dst_col + j2 * 4:bq_dst_col + j2 * 4 + 4],
                pA[:, 1536:1540]), inc=1)
            trb_s[0] = P.tick("s")
        return P.tick("s")

    P.wait("v", "g:sm", g_small)
    P.op("v", lambda v: v.tensor_copy(qkvbb_sb[:], qkvb_sb[:, 0:16]), inc=1)
    v_qkvbb = P.tick("v")
    qkv_bT = qkvbb_sb

    rot_q, emit_rot_q = make_rot_emitter(g_natq, g_rot, rot2_sb)
    _ROT_PLAN = {(0, 1): 2, (1, 0): 2, (1, 1): 2, (2, 0): 2, (2, 1): 2,
                 (3, 0): 3, (3, 1): 3}

    def ln1_hook(tc, stage):
        emit_rot_q(_ROT_PLAN.get((tc, stage), 0))

    P.wait("t", "g:x", g_x)
    P.wait("s", "g:x", g_x)
    P.wait("v", "g:x", g_x)
    P.wait("s", "v", t_mod_done)
    P.wait("t", "v", t_mod_done)
    ln1_s, ln1_v = layernorm(lambda c: x_sb[:, c], lambda c: h_sb[:, c],
                             0, 0, True, range(TC), t1pair=(t1a2_sb, t1b2_sb),
                             hook=ln1_hook)

    if n_phases <= 2:
        return _finish(nc, P, taps, locals())

    emit_rot_q(16 - rot_q["gx"])
    mq_s = rot_q["mw_s"][15]
    bq_s = emit_rot_bias(0, 0, rot2_sb)

    P.wait("g", "t", rot_q["last_t"])
    g_natk = P.load("w0", qnat2_sb[:], d_qkvN.ap()[E:2 * E].rearrange("(c p) k -> p c k", p=128))

    def proj_sweep(lhs_fn, rhs_fn, evac_fn, wait_g, wait_s=0, tc_wait_v=None,
                   guard=(), lead_slB=False):
        gidx = 0
        evac = {}
        for tc in range(TC):
            if tc_wait_v:
                P.wait("t", "v", tc_wait_v[tc])
            for fc in range(CH):
                if lead_slB:
                    slot = slB(gidx) if gidx < 2 else slA((gidx - 2) % 3)
                    prev = gidx - 3 if gidx >= 5 else None
                else:
                    slot = slA(gidx % 3)
                    prev = gidx - 3 if gidx >= 3 else None
                P.wait("t", "g:w0", wait_g)
                P.wait("t", "s", wait_s)
                for gi, src, val in guard:
                    if gi == gidx:
                        P.wait("t", src, val)
                if prev is not None:
                    eng, tk = evac[prev]
                    P.wait("t", eng, tk)
                mm_group(slot, [(lhs_fn(k, fc), rhs_fn(k, tc)) for k in range(CH)])
                evac[gidx] = evac_fn(fc, tc, slot, P.tick("t"))
                gidx += 1
        return evac

    def evac_q(fc, tc, ps, t_tick):
        P.wait("s", "t", t_tick)
        P.wait("s", "s", bq_s)
        P.op("s", lambda s, fc=fc, tc=tc, ps=ps: s.activation(
            out=q_sb[:, fc, tc * 512:(tc + 1) * 512], in_=ps, func=AF.Identity,
            bias=bqr_sb[:, fc:fc + 1]), inc=1)
        return ("s", P.tick("s"))

    q_evac = proj_sweep(lambda k, fc: mw_sb[:, k, fc * 128:(fc + 1) * 128],
                        lambda k, tc: h_sb[:, k, tc * 512:(tc + 1) * 512],
                        evac_q, g_natq, wait_s=mq_s, tc_wait_v=LNS["h_done"],
                        guard=((3, "v", LNS["prev_var"]), (4, "v", LNS["prev_var"])),
                        lead_slB=True)
    t_qproj = P.tick("t")

    rot_k, emit_rot_k = make_rot_emitter(g_natk, g_ent, ent_sb, prev_s=q_evac[1][1],
                                         ch_w="w2")
    emit_rot_k(16)
    mk_s = rot_k["mw_s"][15]
    bk_s = emit_rot_bias(8, 8, ent_sb)

    P.wait("g", "t", rot_k["last_t"])
    g_qkvv = P.load("w0", qkvv_sb[:], d_qkvT.ap()[:, 2 * E:3 * E].rearrange("(c p) f -> p c f", p=128))

    def evac_k(fc, tc, ps, t_tick):
        P.wait("s", "t", t_tick)
        P.wait("s", "s", bk_s)
        P.op("s", lambda s, fc=fc, tc=tc, ps=ps: s.activation(
            out=k_sb[:, fc, tc * 512:(tc + 1) * 512], in_=ps, func=AF.Identity,
            bias=bqr_sb[:, 8 + fc:8 + fc + 1]), inc=1)
        return ("s", P.tick("s"))

    k_evac = proj_sweep(lambda k, fc: mw_sb[:, k, fc * 128:(fc + 1) * 128],
                        lambda k, tc: h_sb[:, k, tc * 512:(tc + 1) * 512],
                        evac_k, g_natk, wait_s=mk_s)
    t_kproj = P.tick("t")

    P.wait("g", "v", ln1_v)
    P.wait("g", "s", ln1_s)
    g_vbrow = P.load("vb", vbrow2_sb[:], d_vb_row.ap())
    P.wait("t", "v", ln1_v)
    P.wait("t", "g:vb", g_vbrow)
    P.op("t", lambda t: t.matmul(slB(2), ones_rb[:], vbrow2_sb[:, 0:512], start=True, stop=True))
    P.op("t", lambda t: t.matmul(slB(3), ones_rb[:], vbrow2_sb[:, 512:1024], start=True, stop=True), inc=1)
    t_vb = P.tick("t")
    P.wait("v", "t", t_vb)
    P.op("v", lambda v: v.tensor_copy(vbias_sb[:, 0:512], slB(2)))
    P.op("v", lambda v: v.tensor_copy(vbias_sb[:, 512:1024], slB(3)), inc=1)
    P.wait("v", "t", t_kproj)
    P.op("v", lambda v: v.memset(v_sb[:], 1.0), inc=1)
    P.dr("v")

    v_evac = {}
    gidx = 0
    k_last_s = k_evac[TC * CH - 1][1]
    for tt in range(TT):
        for fv in range(2):
            slot = slA(gidx % 3)
            P.wait("t", "s", k_last_s)
            P.wait("t", "g:w0", g_qkvv)
            if gidx >= 3:
                P.wait("t", "v", v_evac[gidx - 3])
            mm_group(slot, [(h_sb[:, k, tt * 128:(tt + 1) * 128],
                             qkvv_sb[:, k, fv * 512:(fv + 1) * 512]) for k in range(CH)])
            tk = P.tick("t")
            P.wait("v", "t", tk)

            def ev(v, tt=tt, fv=fv, slot=slot):
                src = slot.rearrange("p (h d) -> p h d", d=64)
                dst = v_sb[:, tt, fv * 520:(fv + 1) * 520].rearrange(
                    "p (h x) -> p h x", x=65)[:, :, 0:64]
                return v.tensor_add(dst, src, vbias_sb[:, fv * 512:(fv + 1) * 512]
                                    .rearrange("p (h d) -> p h d", d=64))
            P.op("v", ev, inc=1)
            v_evac[gidx] = P.tick("v")
            gidx += 1
    P.wait("t", "s", bk_s)
    P.wait("t", "g:sm", g_esb)
    P.op("t", lambda t: t.matmul(pA[:, 1536:1568], ones_rf[:], esbrow_sb[:],
                                 start=True, stop=True), inc=1)
    t_esb = P.tick("t")
    P.wait("v", "t", t_esb)
    P.op("v", lambda v: v.tensor_copy(esb_sb[:], pA[:, 1536:1568]), inc=1)
    v_esb = P.tick("v")
    P.wait("t", "v", v_esb)

    t_p3 = P.tick("t")
    v_p3 = P.tick("v")
    s_p3 = P.tick("s")

    if n_phases <= 3:
        return _finish(nc, P, taps, locals())

    P.wait("g", "t", t_p3)
    g_outw = P.load("w1", outw_sb[:], d_outwT.ap().rearrange("(c p) f -> p c f", p=128))

    exp_tick = {}
    qk_tick = {}
    pv_tick = {}
    mult_tick = 0
    oproj_s = {}
    ln2_sv = {}
    resid_last = {}
    bcn_tick = [0]
    P.wait("t", "v", v_p3)
    P.wait("t", "s", s_p3)

    for qc in range(TC):
        if qc > 0:
            P.wait("t", "v", resid_last[qc - 1])
        for hp in range(CH):
            h0, h1 = 2 * hp, 2 * hp + 1
            LAG = 2

            def emit_qk(kt, hp=hp, qc=qc):
                seq = (qc * CH + hp) * TT + kt
                ssl = (pA[:, 0:1024], pA[:, 1024:2048], pB[:, 1024:2048])[seq % 3]
                if seq >= 3:
                    P.wait("t", "s", exp_tick.get(seq - 3, 0))
                if seq % 3 == 2 and qc > 0 and hp == 0 and kt <= 2:
                    P.wait("t", "v", ln2_sv[qc - 1][1])
                if kt == 0:
                    P.wait("t", "v", mult_tick)
                P.op("t", lambda t, hp=hp, kt=kt, qc=qc, ssl=ssl: t.matmul(
                    ssl[:, 0:512], k_sb[0:64, hp, kt * 128:(kt + 1) * 128],
                    q_sb[0:64, hp, qc * 512:(qc + 1) * 512],
                    start=True, stop=True))
                P.op("t", lambda t, hp=hp, kt=kt, qc=qc, ssl=ssl: t.matmul(
                    ssl[:, 512:1024], k_sb[64:128, hp, kt * 128:(kt + 1) * 128],
                    q_sb[64:128, hp, qc * 512:(qc + 1) * 512],
                    start=True, stop=True), inc=1)
                qk_tick[seq] = P.tick("t")
                P.wait("s", "t", qk_tick[seq])
                P.wait("s", "t", pv_tick.get(seq - 4, 0))
                P.op("s", lambda s, ssl=ssl, pslot=seq % 4: s.activation(
                    out=pt_sb[:, pslot, :], in_=ssl,
                    func=AF.Exp, scale=0.125), inc=1)
                exp_tick[seq] = P.tick("s")

            def emit_pv(kt, hp=hp, qc=qc, h0=h0, h1=h1):
                seq = (qc * CH + hp) * TT + kt
                P.wait("t", "s", exp_tick[seq])
                P.op("t", lambda t, h0=h0, kt=kt, pslot=seq % 4: t.matmul(
                    pB[0:65, 0:512], v_sb[:, kt, h0 * 65:h0 * 65 + 65],
                    pt_sb[:, pslot, 0:512], start=(kt == 0), stop=(kt == TT - 1)))
                P.op("t", lambda t, h1=h1, kt=kt, pslot=seq % 4: t.matmul(
                    pB[0:65, 512:1024], v_sb[:, kt, h1 * 65:h1 * 65 + 65],
                    pt_sb[:, pslot, 512:1024], start=(kt == 0), stop=(kt == TT - 1)),
                    inc=1 if kt % 2 == 1 else 0)
                if kt % 2 == 1:
                    tkk = P.tick("t")
                    pv_tick[seq - 1] = pv_tick[seq] = tkk

            for kt in range(TT):
                emit_qk(kt)
                if kt >= LAG:
                    emit_pv(kt - LAG)
            for kt in range(TT - LAG, TT):
                emit_pv(kt)
            P.wait("s", "t", pv_tick[(qc * CH + hp) * TT + TT - 1])
            P.op("s", lambda s: s.activation(out=rc_sb[:], in_=pB[64:65, 0:1024],
                                             func=AF.Ln), inc=1)
            P.dr("s")
            P.op("s", lambda s: s.activation(out=rcbf_sb[:], in_=rc_sb[:],
                                             func=AF.Exp, scale=-1.0), inc=1)
            t_rc = P.tick("s")
            P.wait("t", "s", t_rc)
            P.op("t", lambda t: t.matmul(pB[0:64, 1024:1536], ones_rb[0:1, 0:64],
                                         rcbf_sb[0:1, 0:512], start=True, stop=True))
            P.op("t", lambda t: t.matmul(pB[0:64, 1536:2048], ones_rb[0:1, 0:64],
                                         rcbf_sb[0:1, 512:1024], start=True, stop=True), inc=1)
            t_bcn = P.tick("t")
            P.wait("v", "t", t_bcn)
            P.op("v", lambda v: v.tensor_copy(
                bcn2_sb[0:64, :, :], pB[0:64, 1024:2048].rearrange("p (c f) -> p c f", c=2)), inc=1)
            P.dr("v")
            P.op("v", lambda v, hp=hp: v.tensor_mul(
                att_sb[0:64, hp, :], pB[0:65, 0:512][0:64, :], bcn2_sb[0:64, 0, :]))
            P.op("v", lambda v, hp=hp: v.tensor_mul(
                att_sb[64:128, hp, :], pB[0:65, 512:1024][0:64, :],
                bcn2_sb[0:64, 1, :]), inc=1)
            mult_tick = P.tick("v")

        resid_v = {}
        last_exp = exp_tick[(qc * CH + CH) * TT - 1]
        for fc in range(CH):
            oslot = pA[:, 1536:2048] if fc % 2 == 0 else pA[:, 0:512]
            P.wait("t", "v", mult_tick)
            P.wait("t", "g:w1", g_outw)
            P.wait("t", "s", last_exp)
            prev = resid_v.get(fc - 2, 0)
            P.wait("t", "v", prev)
            mm_group(oslot, [(outw_sb[:, e, fc * 128:(fc + 1) * 128],
                              att_sb[:, e, :]) for e in range(CH)])
            tk = P.tick("t")
            P.wait("v", "t", tk)
            P.op("v", lambda v, fc=fc, qc=qc, oslot=oslot: v.scalar_tensor_tensor(
                out=x_sb[:, fc, qc * 512:(qc + 1) * 512], in0=oslot,
                scalar=gg_sb[:, fc:fc + 1],
                in1=x_sb[:, fc, qc * 512:(qc + 1) * 512],
                op0=ALU.mult, op1=ALU.add), inc=1)
            resid_v[fc] = P.tick("v")
            oproj_s[(qc, fc)] = P.tick("s")

        ln2_sv[qc] = layernorm(lambda c: x_sb[:, c], lambda c: h_sb[:, c],
                               8, 24, False, [qc],
                               pre_v_tick=lambda tc, rv=resid_v[CH - 1]: rv,
                               pshome=(pB[0:1, 1024:1536], pB[0:1, 1536:2048],
                                       pB[:, 1024:1536], pB[:, 1536:2048]))
        bcn_tick[0] = P.tick("t")
        resid_last[qc] = resid_v[CH - 1]

    t_p4 = P.tick("t")
    s_p4 = P.tick("s")
    v_p4 = P.tick("v")

    if n_phases <= 4:
        return _finish(nc, P, taps, locals())

    P.wait("g", "t", t_p4)
    P.wait("g", "s", s_p4)
    P.wait("g", "v", v_p4)
    g_sgw = P.load("w0", sgw_sb[:], d_sgwT.ap().rearrange("(c p) f -> p c f", p=128))
    g_suw = P.load("w1", suw_sb[:], d_suwT.ap().rearrange("(c p) f -> p c f", p=128))
    sdw2_sb = at("sdw2_sb", [128, CH, E], BF16, R_V + 16384)
    g_sdw2 = P.load("w2", sdw2_sb[:], d_sdwT.ap().rearrange("(c p) f -> p c f", p=128))

    P.wait("t", "s", s_p4)
    P.wait("t", "v", v_p4)

    rt_evac = {}

    def emit_router_mm(tt):
        slot = pA[:, (tt % 4) * 512: (tt % 4) * 512 + NE]
        P.wait("t", "g:sm", g_rwt)
        P.wait("t", "v", v_esb)
        P.wait("t", "v", rt_evac.get(tt - 4, 0))
        mm_group(slot, [(h_sb[:, k, tt * 128:(tt + 1) * 128], rwt_sb[:, k, :])
                        for k in range(CH)])
        tk = P.tick("t")
        P.wait("v", "t", tk)
        P.op("v", lambda v, tt=tt, slot=slot: v.tensor_copy(lbuf_sb[:, tt, :], slot), inc=1)
        rt_evac[tt] = P.tick("v")

    def emit_router_math(tt):
        P.wait("s", "v", rt_evac[tt])
        P.op("s", lambda s, tt=tt: s.activation(
            out=comb_sb[:, tt, :], in_=lbuf_sb[:, tt, :], func=AF.Exp), inc=1)
        t_exp = P.tick("s")
        P.wait("v", "s", t_exp)
        P.dr("v")
        P.op("v", lambda v, tt=tt: v.reduce_sum(rr_sb[:, 0:1], comb_sb[:, tt, :],
                                                axis=mybir.AxisListType.X))
        P.dr("v")
        P.op("v", lambda v: v.reciprocal(rr_sb[:, 1:2], rr_sb[:, 0:1]))
        P.dr("v")
        P.op("v", lambda v, tt=tt: v.tensor_scalar_mul(rw_sb[:, tt, :], comb_sb[:, tt, :],
                                                       rr_sb[:, 1:2]))
        P.dr("v")
        P.op("v", lambda v, tt=tt: v.tensor_add(lbuf_sb[:, tt, :], rw_sb[:, tt, :], esb_sb[:]))
        P.dr("v")
        P.op("v", lambda v, tt=tt: v.max(out=top8_sb[:], in_=lbuf_sb[:, tt, :]))
        P.dr("v")
        P.op("v", lambda v, tt=tt: v.tensor_scalar(
            comb_sb[:, tt, :], lbuf_sb[:, tt, :], top8_sb[:, 5:6], None, op0=ALU.is_ge))
        P.dr("v")
        P.op("v", lambda v, tt=tt: v.tensor_mul(comb_sb[:, tt, :], comb_sb[:, tt, :],
                                                rw_sb[:, tt, :]))
        P.dr("v")
        P.op("v", lambda v, tt=tt: v.reduce_sum(rr_sb[:, 0:1], comb_sb[:, tt, :],
                                                axis=mybir.AxisListType.X))
        P.dr("v")
        P.op("v", lambda v: v.tensor_scalar_max(rr_sb[:, 1:2], rr_sb[:, 0:1], 1e-12))
        P.dr("v")
        P.op("v", lambda v: v.reciprocal(rr_sb[:, 0:1], rr_sb[:, 1:2]))
        P.dr("v")
        P.op("v", lambda v, tt=tt: v.tensor_scalar_mul(rw_sb[:, tt, :], comb_sb[:, tt, :],
                                                       rr_sb[:, 0:1]), inc=1)
        comb_v[tt] = P.tick("v")

    comb_v = {}
    sh_pair_v = {}
    sh_hsh_tc = {}
    gidx = 0
    for tt in range(TT):
        emit_router_mm(tt)
    for tc in range(TC):
        for ic in range(CH):
            sA_ = slA(2 * (gidx % 2))
            sB_ = slA(2 * (gidx % 2) + 1)
            P.wait("t", "g:w0", g_sgw)
            P.wait("t", "g:w1", g_suw)
            P.wait("t", "v", rt_evac[TT - 1])
            P.wait("t", "v", sh_pair_v.get(gidx - 2, 0))
            mm_group(sA_, [(sgw_sb[:, k, ic * 128:(ic + 1) * 128],
                            h_sb[:, k, tc * 512:(tc + 1) * 512]) for k in range(CH)])
            mm_group(sB_, [(suw_sb[:, k, ic * 128:(ic + 1) * 128],
                            h_sb[:, k, tc * 512:(tc + 1) * 512]) for k in range(CH)])
            tk = P.tick("t")
            P.wait("s", "t", tk)
            P.wait("s", "v", sh_pair_v.get(gidx - 2, 0))
            P.op("s", lambda s, g2=gidx % 2, sA_=sA_: s.activation(
                out=sgt_sb[:, g2, :], in_=sA_, func=AF.Silu), inc=1)
            t_silu = P.tick("s")
            if gidx < TT:
                emit_router_math(gidx)
            P.wait("v", "s", t_silu)
            P.op("v", lambda v, ic=ic, tc=tc, g2=gidx % 2, sB_=sB_: v.tensor_mul(
                hsh_sb[:, ic, tc * 512:(tc + 1) * 512], sgt_sb[:, g2, :], sB_), inc=1)
            sh_pair_v[gidx] = P.tick("v")
            gidx += 1
        sh_hsh_tc[tc] = P.tick("v")
    v_hsh = P.tick("v")
    t_sg_done = P.tick("t")

    sd_evac = {}
    gidx = 0
    for tt in range(TT):
        for hc in range(2):
            slot = slB(gidx % 4)
            P.wait("t", "g:w2", g_sdw2)
            P.wait("t", "v", sh_hsh_tc[tt // 4])
            P.wait("t", "v", sd_evac.get(gidx - 4, 0))
            mm_group(slot, [(hsh_sb[:, ic, tt * 128:(tt + 1) * 128],
                             sdw2_sb[:, ic, hc * 512:(hc + 1) * 512]) for ic in range(CH)])
            tk = P.tick("t")
            P.wait("v", "t", tk)
            P.op("v", lambda v, tt=tt, hc=hc, slot=slot: v.tensor_copy(
                y_sb[:, tt, hc * 512:(hc + 1) * 512], slot), inc=1)
            sd_evac[gidx] = P.tick("v")
            gidx += 1
    t_p5 = P.tick("t")
    v_p5 = P.tick("v")
    s_p5 = P.tick("s")

    if n_phases <= 5:
        return _finish(nc, P, taps, locals())

    gu_pair_v = {}
    y_evac_v = {}
    down_t = {}
    gu_t = {}
    for e in range(NE):
        P.wait("g", "t", gu_t.get(e - 1, t_sg_done))
        g_gup = P.load("gup", gup_sb[:], d_gupT.ap()[e].rearrange("(c p) f -> p c f", p=128))
        P.wait("g", "t", down_t.get(e - 1, t_sg_done))
        g_down = P.load("dwn", down_sb[:], d_downT.ap()[e].rearrange("(c p) f -> p c f", p=128))

        didx = [0]

        def emit_gu(tc, e=e, g_gup=g_gup):
            for i in range(4):
                gidx = tc * 4 + i
                sg_ = slA(2 * (gidx % 2))
                su_ = slA(2 * (gidx % 2) + 1)
                P.wait("t", "g:gup", g_gup)
                prev_pair = gu_pair_v.get((e, gidx - 2), gu_pair_v.get((e - 1, gidx + 14),
                                                                       v_p5))
                P.wait("t", "v", prev_pair)
                mm_group(sg_, [(gup_sb[:, k, i * 128:(i + 1) * 128],
                                h_sb[:, k, tc * 512:(tc + 1) * 512]) for k in range(CH)])
                mm_group(su_, [(gup_sb[:, k, 512 + i * 128:512 + (i + 1) * 128],
                                h_sb[:, k, tc * 512:(tc + 1) * 512]) for k in range(CH)])
                tk = P.tick("t")
                P.wait("s", "t", tk)
                P.wait("s", "v", prev_pair)
                P.op("s", lambda s, g2=gidx % 2, sg_=sg_: s.activation(
                    out=sgt_sb[:, g2, :], in_=sg_, func=AF.Silu), inc=1)
                t_silu = P.tick("s")
                P.wait("v", "s", t_silu)
                P.wait("v", "t", down_t.get((e, tc - 2), down_t.get((e - 1, tc + 2), 0)))
                P.op("v", lambda v, i=i, tcp=tc % 2, g2=gidx % 2, su_=su_: v.tensor_mul(
                    hmid_sb[:, tcp, i, :], sgt_sb[:, g2, :], su_), inc=1)
                gu_pair_v[(e, gidx)] = P.tick("v")

        def emit_down(tc, e=e, g_down=g_down):
            hm_ready = gu_pair_v[(e, tc * 4 + 3)]
            for ti in range(4):
                tt = tc * 4 + ti
                for hc in range(2):
                    slot = slB(didx[0] % 4)
                    P.wait("t", "g:dwn", g_down)
                    P.wait("t", "v", hm_ready)
                    P.wait("t", "v", y_evac_v.get((e, didx[0] - 4),
                                                  y_evac_v.get((e - 1, didx[0] + 28), v_p5)))
                    mm_group(slot, [(hmid_sb[:, tc % 2, ic, ti * 128:(ti + 1) * 128],
                                     down_sb[:, ic, hc * 512:(hc + 1) * 512])
                                    for ic in range(4)])
                    tk = P.tick("t")
                    P.wait("v", "t", tk)
                    P.wait("v", "v", comb_v[tt])
                    P.wait("v", "v", sd_evac.get(2 * tt + hc, 0))
                    P.op("v", lambda v, e=e, tt=tt, hc=hc, slot=slot: v.scalar_tensor_tensor(
                        out=y_sb[:, tt, hc * 512:(hc + 1) * 512], in0=slot,
                        scalar=rw_sb[:, tt, e:e + 1],
                        in1=y_sb[:, tt, hc * 512:(hc + 1) * 512],
                        op0=ALU.mult, op1=ALU.add), inc=1)
                    y_evac_v[(e, didx[0])] = P.tick("v")
                    didx[0] += 1
            down_t[(e, tc)] = P.tick("t")

        for tc in range(TC):
            emit_gu(tc)
            if tc == TC - 1:
                gu_t[e] = P.tick("t")
            if tc >= 1:
                emit_down(tc - 1)
        emit_down(TC - 1)
        down_t[e] = P.tick("t")
    v_p6 = P.tick("v")
    t_p6 = P.tick("t")

    if n_phases <= 6:
        return _finish(nc, P, taps, locals())

    P.wait("t", "g:sm", g_eye)
    P.wait("t", "v", v_p6)
    P.wait("s", "v", v_p6)
    y_out = 0
    tr_s = {}
    v_fill = {}
    for hc in range(CH):
        for sl in range(4):
            pslot = slB((hc * 4 + sl) % 4, 256).bitcast(BF16)
            gsl = hc * 4 + sl
            prev = tr_s.get((hc, sl - 4), tr_s.get((hc - 1, sl), 0))
            P.wait("t", "s", prev)
            for j in range(4):
                tt = sl * 4 + j
                P.op("t", lambda t, tt=tt, hc=hc, j=j, pslot=pslot: t.transpose(
                    pslot[:, j * 128:(j + 1) * 128],
                    y_sb[:, tt, hc * 128:(hc + 1) * 128], eye_sb[:]),
                    inc=1 if j == 3 else 0)
            tk = P.tick("t")
            P.wait("s", "t", tk)
            t1 = (t1a_sb, t1b_sb)[sl % 2]
            P.wait("s", "v", v_fill.get((hc, sl - 2), v_fill.get((hc - 1, sl + 2), 0)))
            P.op("s", lambda s, hc=hc, pslot=pslot, t1=t1: s.activation(
                out=t1[:], in_=pslot, func=AF.Identity,
                scale=gg_sb[:, 8 + hc:8 + hc + 1]), inc=1)
            tr_s[(hc, sl)] = P.tick("s")
            P.wait("v", "s", tr_s[(hc, sl)])
            if hc >= 2:
                P.wait("v", "g:y" if hc % 2 == 0 else "g:y2", 16 * (hc // 2))
            P.op("v", lambda v, hc=hc, sl=sl, t1=t1: v.tensor_add(
                ostage_sb[:, hc % 2, sl * 512:(sl + 1) * 512],
                x_sb[:, hc, sl * 512:(sl + 1) * 512], t1[:]), inc=1)
            v_fill[(hc, sl)] = P.tick("v")
        P.wait("y", "v", v_fill[(hc, 3)])
        P.load("y" if hc % 2 == 0 else "y2",
               d_out.ap()[hc * 128:(hc + 1) * 128, :], ostage_sb[:, hc % 2, :], eng="y")
        y_out = P.cnt["g:y"]

    return _finish(nc, P, taps, locals())


def _finish(nc, P, taps, env):
    tap_list = []
    for name in taps:
        t = env.get(name)
        if t is None:
            continue
        d = nc.dram_tensor("tap_" + name, list(t.shape), t.dtype, kind="ExternalOutput")
        tap_list.append((d, t))
    if tap_list:
        P.wait("y", "t", P.tick("t"))
        P.wait("y", "v", P.tick("v"))
        P.wait("y", "s", P.tick("s"))
        P.wait("y", "g", P.tick("g"))
        for d, t in tap_list:
            P.op("y", lambda y, d=d, t=t: y.dma_start(out=d.ap(), in_=t[:]), inc=16)
    for ch in ("g:y", "g:y2"):
        cv = P.cnt.get(ch, 0)
        if cv:
            P.ops["y"].append(lambda y, c=ch, v=cv, P=P: y.wait_ge(P.sems[c], v))
    yv = P.tick("y")
    if yv:
        P.ops["y"].append(lambda y, v=yv, P=P: y.wait_ge(P.sems["y"], v))

    from contextlib import ExitStack
    es = ExitStack()
    for e in Prog.ENGS:
        P.sems[e] = es.enter_context(nc.semaphore("sem_" + e))
    for ch in Prog.CHANNELS:
        P.sems["g:" + ch] = es.enter_context(nc.semaphore("sdma_" + ch))
    block = es.enter_context(nc.Block())
    with es:
        @block.tensor
        def _(t):
            for f in P.ops["t"]:
                f(t)

        @block.vector
        def _(v):
            for f in P.ops["v"]:
                f(v)

        @block.scalar
        def _(s):
            for f in P.ops["s"]:
                f(s)

        @block.gpsimd
        def _(g):
            for f in P.ops["g"]:
                f(g)

        @block.sync
        def _(y):
            for f in P.ops["y"]:
                f(y)
    return nc


def _stage_inputs(inputs):
    f = np.ascontiguousarray
    shared = {
        "wmod": f(inputs["L0_wmod"]),
        "qkvT": f(inputs["L0_attn_qkv_w"].T),
        "qkvN": f(inputs["L0_attn_qkv_w"][0:2 * E]),
        "rot": f(inputs["L0_attn_rotation_params"]),
        "ent": f(inputs["L0_attn_entangle_params"]),
        "outwT": f(inputs["L0_attn_out_w"].T),
        "qkv_bT": f(inputs["L0_attn_qkv_b"].reshape(24, 128).T),
        "vb_row": f(inputs["L0_attn_qkv_b"][2048:3072].reshape(1, 1024)),
        "out_bT": f(inputs["L0_attn_out_b"].reshape(8, 128).T),
        "sgwT": f(inputs["L0_ffn_shared_gate_w"].T),
        "suwT": f(inputs["L0_ffn_shared_up_w"].T),
        "sdwT": f(inputs["L0_ffn_shared_down_w"].T),
        "rwT": f(inputs["L0_ffn_router_weight"].T),
        "esb_row": f(inputs["L0_ffn_e_score_bias"].reshape(1, 32)),
        "gupT": f(inputs["L0_ffn_gate_up_proj"].transpose(0, 2, 1)),
        "downT": f(inputs["L0_ffn_down_proj"].transpose(0, 2, 1)),
    }
    x = np.asarray(inputs["L0_attn_x"])
    c = np.asarray(inputs["L0_c"])
    in_maps = []
    for b in range(8):
        m = dict(shared)
        m["xT"] = f(x[b].T)
        m["cT"] = f(c[b].reshape(8, 128).T)
        in_maps.append(m)
    return in_maps


def kernel(**inputs):
    inputs = {k: np.asarray(v, dtype=np.float32) for k, v in inputs.items()}
    in_maps = _stage_inputs(inputs)
    nc = build_program(n_phases=_N_PHASES, taps=_TAPS)
    res = run_bass_kernel_spmd(nc, in_maps, core_ids=list(range(8)))
    out = np.stack([np.ascontiguousarray(np.asarray(r["outT"]).T) for r in res.results])
    return out.astype(np.float32)


if __name__ == "__main__":
    import reference as _ref
    ins = {k: np.asarray(v) for k, v in _ref.setup_inputs().items()}
    out = kernel(**ins)
    ref = np.asarray(_ref.reference(**ins))
    err = np.mean((out - ref) ** 2) / np.mean(ref ** 2)
    print("residual variance ratio:", err)


# version 20
# speedup vs baseline: 1.1885x; 1.1885x over previous
import numpy as np
import ml_dtypes
import concourse.bass as bass
import concourse.mybir as mybir
from concourse.bass_utils import run_bass_kernel_spmd

F32 = mybir.dt.float32
BF16 = mybir.dt.bfloat16
AF = mybir.ActivationFunctionType
ALU = mybir.AluOpType

E = 1024
S = 2048
CH = 8
TC = 4
TT = 16
NH = 16
NE = 32
EI = 512

_TAPS = []
_N_PHASES = 99


class Prog:

    ENGS = ("t", "v", "s", "g", "y")
    CHANNELS = ("sm", "x", "wm", "wm2", "w0", "w1", "w2", "vb", "gup", "dwn", "y", "y2")

    def __init__(self, nc):
        self.nc = nc
        self.ops = {e: [] for e in self.ENGS}
        self.cnt = {e: 0 for e in self.ENGS}
        for ch in self.CHANNELS:
            self.cnt["g:" + ch] = 0
        self.lastwait = {}
        self.sems = {}

    def op(self, eng, f, inc=None):
        if inc is None:
            inc = 1 if eng in ("v", "s") else 0
        if inc:
            self.cnt[eng] += inc

            def g(E, f=f, name=eng, n=inc, self=self):
                f(E).then_inc(self.sems[name], n)

            self.ops[eng].append(g)
        else:
            self.ops[eng].append(f)
        return self.cnt[eng]

    def wait(self, eng, src, val):
        if val is None or val <= 0:
            return
        key = (eng, src)
        if self.lastwait.get(key, 0) >= val:
            return
        self.lastwait[key] = val

        def g(E, src=src, v=val, self=self):
            E.wait_ge(self.sems[src], v)

        self.ops[eng].append(g)

    def tick(self, eng):
        return self.cnt[eng]

    def load(self, ch, dst, src, eng="g"):
        key = "g:" + ch
        self.cnt[key] += 16

        def f(E, d=dst, s=src, key=key, self=self):
            E.dma_start(out=d, in_=s).then_inc(self.sems[key], 16)

        self.ops[eng].append(f)
        return self.cnt[key]

    def dr(self, eng):
        self.wait(eng, eng, self.cnt[eng])


def build_program(n_phases=99, taps=()):
    nc = bass.Bass("TRN2", target_bir_lowering=False)

    d_xT = nc.dram_tensor("xT", [E, S], F32, kind="ExternalInput")
    d_cT = nc.dram_tensor("cT", [128, 8], F32, kind="ExternalInput")
    d_wmod = nc.dram_tensor("wmod", [E, 6 * E], F32, kind="ExternalInput")
    d_qkvT = nc.dram_tensor("qkvT", [E, 3 * E], F32, kind="ExternalInput")
    d_qkvN = nc.dram_tensor("qkvN", [2 * E, E], F32, kind="ExternalInput")
    d_rot = nc.dram_tensor("rot", [E, E], F32, kind="ExternalInput")
    d_ent = nc.dram_tensor("ent", [E, E], F32, kind="ExternalInput")
    d_outwT = nc.dram_tensor("outwT", [E, E], F32, kind="ExternalInput")
    d_qkv_bT = nc.dram_tensor("qkv_bT", [128, 24], F32, kind="ExternalInput")
    d_vb_row = nc.dram_tensor("vb_row", [1, E], F32, kind="ExternalInput")
    d_out_bT = nc.dram_tensor("out_bT", [128, 8], F32, kind="ExternalInput")
    d_sgwT = nc.dram_tensor("sgwT", [E, E], F32, kind="ExternalInput")
    d_suwT = nc.dram_tensor("suwT", [E, E], F32, kind="ExternalInput")
    d_sdwT = nc.dram_tensor("sdwT", [E, E], F32, kind="ExternalInput")
    d_rwT = nc.dram_tensor("rwT", [E, NE], F32, kind="ExternalInput")
    d_esb = nc.dram_tensor("esb_row", [1, NE], F32, kind="ExternalInput")
    d_gupT = nc.dram_tensor("gupT", [NE, E, E], F32, kind="ExternalInput")
    d_downT = nc.dram_tensor("downT", [NE, EI, E], F32, kind="ExternalInput")
    d_out = nc.dram_tensor("outT", [E, S], F32, kind="ExternalOutput")

    d_eye = nc.inline_tensor(np.eye(128, dtype=ml_dtypes.bfloat16), "eye_inl")

    AR = 208128
    arena = nc.alloc_sbuf_tensor("arena", [128, AR], mybir.dt.uint8)
    base = nc.lookup_mloc(arena).addr

    def at(name, shape, dtype, off):
        return nc.alloc_sbuf_tensor_at(name, shape, dtype, offset=base + off)

    R_X, R_H, R_Q, R_K, R_V, R_W = 0, 32768, 65536, 98304, 131072, 165120
    x_sb = at("x_sb", [128, CH, S], BF16, R_X)
    h_sb = at("h_sb", [128, CH, S], BF16, R_H)
    q_sb = at("q_sb", [128, CH, S], BF16, R_Q)
    y_sb = at("y_sb", [128, TT, E], BF16, R_Q)
    k_sb = at("k_sb", [128, CH, S], BF16, R_K)
    qstage_sb = at("qstage_sb", [128, 2, CH, 512], BF16, R_K)
    t1a2_sb = at("t1a2_sb", [128, 512], F32, R_K + 16384)
    t1b2_sb = at("t1b2_sb", [128, 512], F32, R_K + 18432)
    hsh_sb = at("hsh_sb", [128, CH, S], BF16, R_K)
    hmid_sb = at("hmid_sb", [128, 2, 4, EI], BF16, R_V)
    sgt_sb = at("sgt_sb", [128, 2, 512], BF16, R_V + 8192)
    lbuf_sb = at("lbuf_sb", [128, TT, NE], F32, R_V + 10240)
    rw_sb = at("rw_sb", [128, TT, NE], F32, R_V + 12288)
    comb_sb = at("comb_sb", [128, TT, NE], F32, R_V + 14336)
    v_sb = at("v_sb", [128, TT, NH * 65], BF16, R_V)
    mw_sb = at("mw_sb", [128, CH, E], BF16, R_V)
    kstage_sb = at("kstage_sb", [128, 2, CH, 512], BF16, R_V)
    ostage_sb = at("ostage_sb", [128, 2, S], F32, R_V + 16384)

    W0, W1, W2 = R_W, R_W + 16384, R_W + 32768
    wms_sb = at("wms_sb", [128, 2, CH, 512], F32, W0)
    qkvq_sb = at("qkvq_sb", [128, CH, E], BF16, W0)
    rot_sb = at("rot_sb", [128, CH, E], BF16, W1)
    qkvk_sb = at("qkvk_sb", [128, CH, E], BF16, W0)
    ent_sb = at("ent_sb", [128, CH, E], BF16, W1)
    qkvv_sb = at("qkvv_sb", [128, CH, E], BF16, W0)
    outw_sb = at("outw_sb", [128, CH, E], BF16, W1)
    pt_sb = at("pt_sb", [128, 4, 1024], BF16, W0)
    att_sb = at("att_sb", [128, CH, 512], BF16, W0 + 8192)
    t1a_sb = at("t1a_sb", [128, 512], F32, W0 + 8192)
    t1b_sb = at("t1b_sb", [128, 512], F32, W0 + 12288)
    sgw_sb = at("sgw_sb", [128, CH, E], BF16, W0)
    suw_sb = at("suw_sb", [128, CH, E], BF16, W1)
    sdw_sb = at("sdw_sb", [128, CH, E], BF16, W1)
    gup_sb = at("gup_sb", [128, CH, E], BF16, W0)
    down_sb = at("down_sb", [128, 4, E], BF16, W1)
    r1_sb = at("r1_sb", [1, 512], F32, W2)
    r2_sb = at("r2_sb", [1, 512], F32, W2 + 2048)
    r3_sb = at("r3_sb", [1, 512], F32, W2 + 4096)
    rbf1_sb = at("rbf1_sb", [1, 512], BF16, W2 + 6144)
    rbf2_sb = at("rbf2_sb", [1, 512], BF16, W2 + 7168)
    rc_sb = at("rc_sb", [1, 1024], F32, W2)
    rcbf_sb = at("rcbf_sb", [1, 1024], BF16, W2 + 6144)
    sq_sb = at("sq_sb", [128, 2, 512], BF16, W2 + 8192)
    vbias_sb = at("vbias_sb", [128, E], BF16, W2)
    vbrow2_sb = at("vbrow2_sb", [1, E], BF16, W2 + 2048)
    bcn2_sb = at("bcn2_sb", [128, 2, 512], BF16, W2 + 8192)

    eye_sb = nc.alloc_sbuf_tensor("eye_sb", [128, 128], BF16)
    ones_rb = nc.alloc_sbuf_tensor("ones_rb", [1, 128], BF16)
    ones_rf = nc.alloc_sbuf_tensor("ones_rf", [1, 128], F32)
    ones_cb = nc.alloc_sbuf_tensor("ones_cb", [128, 1], BF16)
    one1f_sb = nc.alloc_sbuf_tensor("one1f_sb", [1, 1], F32)
    kc_sb = nc.alloc_sbuf_tensor("kc_sb", [128, 3], F32)
    ct_sb = nc.alloc_sbuf_tensor("ct_sb", [128, 8], F32)
    cs_sb = nc.alloc_sbuf_tensor("cs_sb", [128, 8], F32)
    qkvb_sb = nc.alloc_sbuf_tensor("qkvb_sb", [128, 24], F32)
    outb_sb = nc.alloc_sbuf_tensor("outb_sb", [128, 8], F32)

    esbrow_sb = nc.alloc_sbuf_tensor("esbrow_sb", [1, NE], F32)
    esb_sb = nc.alloc_sbuf_tensor("esb_sb", [128, NE], F32)
    rwt_sb = nc.alloc_sbuf_tensor("rwt_sb", [128, CH, NE], BF16)
    macc_sb = nc.alloc_sbuf_tensor("macc_sb", [128, 48], F32)
    deriv_sb = nc.alloc_sbuf_tensor("deriv_sb", [128, 16], F32)
    gg_sb = nc.alloc_sbuf_tensor("gg_sb", [128, 16], F32)
    gbo_sb = nc.alloc_sbuf_tensor("gbo_sb", [128, 8], F32)
    bqr_sb = nc.alloc_sbuf_tensor("bqr_sb", [128, 16], F32)
    qkvbb_sb = nc.alloc_sbuf_tensor("qkvbb_sb", [128, 16], BF16)
    tmp16_sb = nc.alloc_sbuf_tensor("tmp16_sb", [128, 16], F32)
    tmp16b_sb = nc.alloc_sbuf_tensor("tmp16b_sb", [128, 16], F32)
    top8_sb = nc.alloc_sbuf_tensor("top8_sb", [128, 8], F32)
    rr_sb = nc.alloc_sbuf_tensor("rr_sb", [128, 2], F32)


    pA = nc.alloc_psum_tensor("pA", [128, 2048], F32)
    pB = nc.alloc_psum_tensor("pB", [128, 2048], F32)

    P = Prog(nc)

    def slA(i, w=512):
        return pA[:, i * 512:i * 512 + w]

    def slB(i, w=512):
        return pB[:, i * 512:i * 512 + w]

    def mm_group(out_ap, terms, inc=1):
        def f(t, out=out_ap, terms=tuple(terms)):
            n = len(terms)
            mm = None
            for i, (l, r) in enumerate(terms):
                mm = t.matmul(out, l, r, start=(i == 0), stop=(i == n - 1))
            return mm
        return P.op("t", f, inc=inc)

    P.op("v", lambda v: v.memset(ones_rb[:], 1.0))
    P.op("v", lambda v: v.memset(ones_rf[:], 1.0))
    P.op("v", lambda v: v.memset(ones_cb[:], 1.0))
    P.op("v", lambda v: v.memset(kc_sb[:, 0:1], 2.0))
    P.op("v", lambda v: v.memset(kc_sb[:, 1:2], -1.0))
    P.op("v", lambda v: v.memset(kc_sb[:, 2:3], 1e-5))
    P.op("v", lambda v: v.memset(one1f_sb[:], 1.0), inc=1)
    v_ones_const = P.tick("v")

    P.load("sm", eye_sb[:], d_eye.ap())
    P.load("sm", ct_sb[:], d_cT.ap())
    P.load("sm", qkvb_sb[:], d_qkv_bT.ap())
    P.load("sm", outb_sb[:], d_out_bT.ap())
    P.load("sm", esbrow_sb[:], d_esb.ap())
    g_small = P.load("sm", rwt_sb[:], d_rwT.ap().rearrange("(c p) f -> p c f", p=128))
    g_eye = g_ct = g_esb = g_rwt = g_small
    g_x = P.load("x", x_sb[:], d_xT.ap().rearrange("(c p) t -> p c t", p=128))

    P.wait("s", "g:sm", g_ct)
    P.op("s", lambda s: s.activation(out=cs_sb[:], in_=ct_sb[:], func=AF.Silu), inc=1)
    t_silu_c = P.tick("s")

    mmg_t = {}
    tr_t = {}
    r1c_v = {}
    macc_v = {}
    for s in range(12):
        ch, qe = ("wm", "y") if s % 2 == 0 else ("wm2", "s")
        P.wait(qe, "t", mmg_t.get(s - 2, 0))
        gt = P.load(ch, wms_sb[:, s % 2],
                    d_wmod.ap()[:, s * 512:(s + 1) * 512].rearrange("(c p) f -> p c f", p=128),
                    eng=qe)
        P.wait("t", "g:" + ch, gt)
        P.wait("t", "s", t_silu_c)
        P.wait("t", "v", r1c_v.get(s - 2, 0))
        rowp = pA[0:1, (s % 2) * 512:(s % 2) * 512 + 512]
        def f(t, s=s, rowp=rowp):
            mm = None
            for k in range(CH):
                mm = t.matmul(rowp, cs_sb[:, k:k + 1], wms_sb[:, s % 2, k, :],
                              start=(k == 0), stop=(k == CH - 1))
            return mm
        P.op("t", f, inc=1)
        mmg_t[s] = P.tick("t")
        P.wait("v", "t", mmg_t[s])
        P.wait("v", "t", tr_t.get(s - 1, 0))
        P.op("v", lambda v, rowp=rowp: v.tensor_copy(r1_sb[:], rowp), inc=1)
        r1c_v[s] = P.tick("v")
        P.wait("t", "v", r1c_v[s])
        P.wait("t", "v", macc_v.get(s - 2, 0))
        trp = pA[:, 1024 + (s % 2) * 512:1024 + (s % 2) * 512 + 4]
        def f2(t, trp=trp):
            mm = None
            for b in range(4):
                mm = t.transpose(trp[:, b:b + 1], r1_sb[0:1, b * 128:(b + 1) * 128],
                                 one1f_sb[:])
            return mm
        P.op("t", f2, inc=1)
        tr_t[s] = P.tick("t")
        P.wait("v", "t", tr_t[s])
        P.op("v", lambda v, s=s, trp=trp: v.tensor_copy(macc_sb[:, s * 4:s * 4 + 4], trp), inc=1)
        macc_v[s] = P.tick("v")
    tick_mod_v = P.tick("v")
    t_mod_mm = mmg_t[11]

    P.dr("v")
    P.op("v", lambda v: v.tensor_scalar_add(deriv_sb[:, 0:8], macc_sb[:, 8:16], 1.0))
    P.op("v", lambda v: v.tensor_scalar_add(deriv_sb[:, 8:16], macc_sb[:, 32:40], 1.0), inc=1)
    t_deriv0 = P.tick("v")
    P.wait("s", "v", t_deriv0)
    P.op("s", lambda s: s.activation(out=tmp16_sb[:, 0:8], in_=macc_sb[:, 16:24], func=AF.Abs))
    P.op("s", lambda s: s.activation(out=tmp16_sb[:, 8:16], in_=macc_sb[:, 40:48], func=AF.Abs), inc=1)
    t_abs = P.tick("s")
    P.wait("v", "s", t_abs)
    P.op("v", lambda v: v.tensor_scalar_max(tmp16b_sb[:], tmp16_sb[:], 0.25))
    P.op("v", lambda v: v.tensor_scalar(gg_sb[:, 0:8], macc_sb[:, 16:24], 0.0, None, op0=ALU.is_ge))
    P.op("v", lambda v: v.tensor_scalar(gg_sb[:, 8:16], macc_sb[:, 40:48], 0.0, None, op0=ALU.is_ge))
    P.dr("v")
    P.op("v", lambda v: v.scalar_tensor_tensor(
        out=tmp16_sb[:], in0=gg_sb[:], scalar=kc_sb[:, 0:1], in1=tmp16b_sb[:],
        op0=ALU.mult, op1=ALU.mult))
    P.dr("v")
    P.op("v", lambda v: v.tensor_sub(gg_sb[:], tmp16_sb[:], tmp16b_sb[:]))
    P.dr("v")
    P.wait("v", "g:sm", g_small)
    P.op("v", lambda v: v.tensor_mul(gbo_sb[:], gg_sb[:, 0:8], outb_sb[:]), inc=1)
    t_mod_done = P.tick("v")

    if n_phases <= 1:
        return _finish(nc, P, taps, locals())

    LNS = {"prev_var": 0, "h_done": {}}

    def layernorm(src_get, dst_get, onep_col, shift_col, fold_ga, tc_range,
                  pre_v_tick=None, t1pair=None, pshome=None, hook=None):
        s_last = v_last = 0
        for tc in tc_range:
            ts = slice(tc * 512, (tc + 1) * 512)
            vpre = pre_v_tick(tc) if pre_v_tick else None
            if vpre:
                P.wait("s", "v", vpre)
                P.wait("t", "v", vpre)
            P.wait("t", "v", LNS["prev_var"])
            s1 = pshome[0] if pshome else pA[0:1, 512:1024]
            s2 = pshome[1] if pshome else pA[0:1, 1024:1536]
            mm_group(s1, [(ones_cb[:], src_get(c)[:, ts]) for c in range(CH)], inc=1)
            sq_t = [0, 0]
            for c in range(CH):
                if c >= 2:
                    P.wait("v", "t", sq_t[c % 2])
                P.op("v", lambda v, c=c, ts=ts: v.tensor_mul(
                    sq_sb[:, c % 2, :], src_get(c)[:, ts], src_get(c)[:, ts]), inc=1)
                tk = P.tick("v")
                P.wait("t", "v", tk)
                P.op("t", lambda t, c=c, s2=s2: t.matmul(
                    s2, ones_cb[:], sq_sb[:, c % 2, :],
                    start=(c == 0), stop=(c == CH - 1)), inc=1)
                sq_t[c % 2] = P.tick("t")
            t_s2 = P.tick("t")
            if hook:
                hook(tc, 0)
            P.wait("v", "t", t_s2)
            P.op("v", lambda v, s1=s1: v.tensor_scalar_mul(r1_sb[:], s1, 1.0 / E))
            P.op("v", lambda v, s2=s2: v.tensor_scalar_mul(r2_sb[:], s2, 1.0 / E))
            P.dr("v")
            P.op("v", lambda v: v.tensor_mul(r3_sb[:], r1_sb[:], r1_sb[:]))
            P.dr("v")
            P.op("v", lambda v: v.tensor_sub(r2_sb[:], r2_sb[:], r3_sb[:]), inc=1)
            t_var = P.tick("v")
            LNS["prev_var"] = t_var
            P.wait("s", "v", t_var)
            P.op("s", lambda s: s.activation(out=r3_sb[:], in_=r2_sb[:], func=AF.Ln,
                                             bias=kc_sb[0:1, 2:3]), inc=1)
            P.dr("s")
            P.op("s", lambda s: s.activation(out=r3_sb[:], in_=r3_sb[:], func=AF.Exp,
                                             scale=-0.5), inc=1)
            t_rstd = P.tick("s")
            P.wait("v", "s", t_rstd)
            P.op("v", lambda v: v.scalar_tensor_tensor(
                out=rbf2_sb[:], in0=r1_sb[:], scalar=kc_sb[0:1, 1:2], in1=r3_sb[:],
                op0=ALU.mult, op1=ALU.mult))
            P.op("v", lambda v: v.tensor_copy(rbf1_sb[:], r3_sb[:]), inc=1)
            t_rows = P.tick("v")
            P.wait("t", "v", t_rows)
            rb, nb = (pshome[2], pshome[3]) if pshome else (slB(2), slB(3))
            P.op("t", lambda t, rb=rb: t.matmul(rb, ones_rb[:], rbf1_sb[:], start=True, stop=True))
            P.op("t", lambda t, nb=nb: t.matmul(nb, ones_rb[:], rbf2_sb[:], start=True, stop=True), inc=1)
            t_bc = P.tick("t")
            if hook:
                hook(tc, 1)
            h_ticks = {}
            for c in range(CH):
                P.wait("v", "t", t_bc)
                t1 = (t1pair or (t1a_sb, t1b_sb))[c % 2]
                P.op("v", lambda v, c=c, ts=ts, t1=t1, rb=rb: v.tensor_mul(
                    t1[:], src_get(c)[:, ts], rb))
                P.dr("v")
                P.op("v", lambda v, t1=t1, nb=nb: v.tensor_add(t1[:], t1[:], nb), inc=1)
                P.dr("v")
                P.op("v", lambda v, c=c, ts=ts, t1=t1: v.tensor_scalar(
                    dst_get(c)[:, ts], t1[:],
                    deriv_sb[:, onep_col + c:onep_col + c + 1],
                    macc_sb[:, shift_col + c:shift_col + c + 1],
                    op0=ALU.mult, op1=ALU.add), inc=1)
                h_ticks[c] = P.tick("v")
                if fold_ga:
                    P.dr("v")
                    P.op("v", lambda v, c=c, ts=ts: v.scalar_tensor_tensor(
                        out=x_sb[:, c, ts], in0=h_sb[:, c, ts],
                        scalar=gg_sb[:, c:c + 1], in1=x_sb[:, c, ts],
                        op0=ALU.mult, op1=ALU.add), inc=1)
                    t_xt = P.tick("v")
                    P.wait("s", "v", t_xt)
                    P.op("s", lambda s, c=c, ts=ts: s.activation(
                        out=x_sb[:, c, ts], in_=x_sb[:, c, ts], func=AF.Identity,
                        bias=gbo_sb[:, c:c + 1]), inc=1)
            LNS["h_done"][tc] = h_ticks[CH - 1]
            s_last, v_last = P.tick("s"), P.tick("v")
        return s_last, v_last

    qnat2_sb = at("qnat2_sb", [128, CH, E], BF16, R_K)
    rot2_sb = at("rot2_sb", [128, CH, E], BF16, R_Q)
    g_natq = P.load("w0", qnat2_sb[:], d_qkvN.ap()[0:E].rearrange("(c p) k -> p c k", p=128))
    g_rot = P.load("w1", rot2_sb[:], d_rot.ap().rearrange("(c p) f -> p c f", p=128))
    P.wait("g", "t", t_mod_mm)
    g_ent = P.load("w2", ent_sb[:], d_ent.ap().rearrange("(c p) f -> p c f", p=128))

    brow_sb = at("brow_sb", [1, 512], F32, R_V + 16384)
    trb_t = [0]
    trb_s = [0]

    def make_rot_emitter(g_nat, g_w, w_sb, prev_s=0, ch_w="w1"):
        st = {"gx": 0, "mw_s": {}, "last_t": 0}

        def emit(n):
            for _ in range(n):
                gx = st["gx"]
                kc, j2 = gx // 2, gx % 2
                slot = slB(gx % 2)
                P.wait("t", "g:w0", g_nat)
                P.wait("t", "g:" + ch_w, g_w)
                P.wait("t", "s", st["mw_s"].get(gx - 2, prev_s))
                mm_group(slot, [(qnat2_sb[:, ic, kc * 128:(kc + 1) * 128],
                                 w_sb[:, ic, j2 * 512:(j2 + 1) * 512])
                                for ic in range(CH)])
                tk = P.tick("t")
                st["last_t"] = tk
                P.wait("s", "t", tk)
                P.op("s", lambda s, kc=kc, j2=j2, slot=slot: s.activation(
                    out=mw_sb[:, kc, j2 * 512:(j2 + 1) * 512], in_=slot,
                    func=AF.Identity), inc=1)
                st["mw_s"][gx] = P.tick("s")
                st["gx"] = gx + 1
        return st, emit

    def emit_rot_bias(bias_cols, bq_dst_col, w_sb):
        rowp = pA[0:1, 1536:2048]
        for j2 in range(2):
            P.wait("t", "s", trb_s[0])
            P.wait("t", "v", v_qkvbb)

            def fb(t, j2=j2, rowp=rowp, bias_cols=bias_cols, w_sb=w_sb):
                mm = None
                for ic in range(CH):
                    mm = t.matmul(rowp[:, 0:512], qkv_bT[:, bias_cols + ic:bias_cols + ic + 1],
                                  w_sb[:, ic, j2 * 512:(j2 + 1) * 512],
                                  start=(ic == 0), stop=(ic == CH - 1))
                return mm
            P.op("t", fb, inc=1)
            tk = P.tick("t")
            P.wait("v", "t", tk)
            P.wait("v", "t", trb_t[0])
            P.op("v", lambda v, rowp=rowp: v.tensor_copy(brow_sb[:], rowp[:, 0:512]), inc=1)
            vr = P.tick("v")
            P.wait("t", "v", vr)
            for b in range(4):
                P.op("t", lambda t, b=b: t.transpose(
                    pA[:, 1536 + b:1537 + b], brow_sb[0:1, b * 128:(b + 1) * 128],
                    one1f_sb[:]), inc=1 if b == 3 else 0)
            trb_t[0] = P.tick("t")
            P.wait("s", "t", trb_t[0])
            P.op("s", lambda s, j2=j2, bq_dst_col=bq_dst_col: s.copy(
                bqr_sb[:, bq_dst_col + j2 * 4:bq_dst_col + j2 * 4 + 4],
                pA[:, 1536:1540]), inc=1)
            trb_s[0] = P.tick("s")
        return P.tick("s")

    P.wait("v", "g:sm", g_small)
    P.op("v", lambda v: v.tensor_copy(qkvbb_sb[:], qkvb_sb[:, 0:16]), inc=1)
    v_qkvbb = P.tick("v")
    qkv_bT = qkvbb_sb

    rot_q, emit_rot_q = make_rot_emitter(g_natq, g_rot, rot2_sb)
    _ROT_PLAN = {(0, 1): 2, (1, 0): 2, (1, 1): 2, (2, 0): 2, (2, 1): 2,
                 (3, 0): 3, (3, 1): 3}

    def ln1_hook(tc, stage):
        emit_rot_q(_ROT_PLAN.get((tc, stage), 0))

    P.wait("t", "g:x", g_x)
    P.wait("s", "g:x", g_x)
    P.wait("v", "g:x", g_x)
    P.wait("s", "v", t_mod_done)
    P.wait("t", "v", t_mod_done)
    ln1_s, ln1_v = layernorm(lambda c: x_sb[:, c], lambda c: h_sb[:, c],
                             0, 0, True, range(TC), t1pair=(t1a2_sb, t1b2_sb),
                             hook=ln1_hook)

    if n_phases <= 2:
        return _finish(nc, P, taps, locals())

    emit_rot_q(16 - rot_q["gx"])
    mq_s = rot_q["mw_s"][15]
    bq_s = emit_rot_bias(0, 0, rot2_sb)

    P.wait("g", "t", rot_q["last_t"])
    g_natk = P.load("w0", qnat2_sb[:], d_qkvN.ap()[E:2 * E].rearrange("(c p) k -> p c k", p=128))

    def proj_sweep(lhs_fn, rhs_fn, evac_fn, wait_g, wait_s=0, tc_wait_v=None,
                   guard=(), lead_slB=False):
        gidx = 0
        evac = {}
        for tc in range(TC):
            if tc_wait_v:
                P.wait("t", "v", tc_wait_v[tc])
            for fc in range(CH):
                if lead_slB:
                    slot = slB(gidx) if gidx < 2 else slA((gidx - 2) % 3)
                    prev = gidx - 3 if gidx >= 5 else None
                else:
                    slot = slA(gidx % 3)
                    prev = gidx - 3 if gidx >= 3 else None
                P.wait("t", "g:w0", wait_g)
                P.wait("t", "s", wait_s)
                for gi, src, val in guard:
                    if gi == gidx:
                        P.wait("t", src, val)
                if prev is not None:
                    eng, tk = evac[prev]
                    P.wait("t", eng, tk)
                mm_group(slot, [(lhs_fn(k, fc), rhs_fn(k, tc)) for k in range(CH)])
                evac[gidx] = evac_fn(fc, tc, slot, P.tick("t"))
                gidx += 1
        return evac

    def evac_q(fc, tc, ps, t_tick):
        P.wait("s", "t", t_tick)
        P.wait("s", "s", bq_s)
        P.op("s", lambda s, fc=fc, tc=tc, ps=ps: s.activation(
            out=q_sb[:, fc, tc * 512:(tc + 1) * 512], in_=ps, func=AF.Identity,
            bias=bqr_sb[:, fc:fc + 1]), inc=1)
        return ("s", P.tick("s"))

    q_evac = proj_sweep(lambda k, fc: mw_sb[:, k, fc * 128:(fc + 1) * 128],
                        lambda k, tc: h_sb[:, k, tc * 512:(tc + 1) * 512],
                        evac_q, g_natq, wait_s=mq_s, tc_wait_v=LNS["h_done"],
                        guard=((3, "v", LNS["prev_var"]), (4, "v", LNS["prev_var"])),
                        lead_slB=True)
    t_qproj = P.tick("t")

    rot_k, emit_rot_k = make_rot_emitter(g_natk, g_ent, ent_sb, prev_s=q_evac[1][1],
                                         ch_w="w2")
    emit_rot_k(16)
    mk_s = rot_k["mw_s"][15]
    bk_s = emit_rot_bias(8, 8, ent_sb)

    P.wait("g", "t", rot_k["last_t"])
    g_qkvv = P.load("w0", qkvv_sb[:], d_qkvT.ap()[:, 2 * E:3 * E].rearrange("(c p) f -> p c f", p=128))

    def evac_k(fc, tc, ps, t_tick):
        P.wait("s", "t", t_tick)
        P.wait("s", "s", bk_s)
        P.op("s", lambda s, fc=fc, tc=tc, ps=ps: s.activation(
            out=k_sb[:, fc, tc * 512:(tc + 1) * 512], in_=ps, func=AF.Identity,
            bias=bqr_sb[:, 8 + fc:8 + fc + 1]), inc=1)
        return ("s", P.tick("s"))

    k_evac = proj_sweep(lambda k, fc: mw_sb[:, k, fc * 128:(fc + 1) * 128],
                        lambda k, tc: h_sb[:, k, tc * 512:(tc + 1) * 512],
                        evac_k, g_natk, wait_s=mk_s)
    t_kproj = P.tick("t")

    P.wait("g", "v", ln1_v)
    P.wait("g", "s", ln1_s)
    g_vbrow = P.load("vb", vbrow2_sb[:], d_vb_row.ap())
    P.wait("t", "v", ln1_v)
    P.wait("t", "g:vb", g_vbrow)
    P.op("t", lambda t: t.matmul(slB(2), ones_rb[:], vbrow2_sb[:, 0:512], start=True, stop=True))
    P.op("t", lambda t: t.matmul(slB(3), ones_rb[:], vbrow2_sb[:, 512:1024], start=True, stop=True), inc=1)
    t_vb = P.tick("t")
    P.wait("v", "t", t_vb)
    P.op("v", lambda v: v.tensor_copy(vbias_sb[:, 0:512], slB(2)))
    P.op("v", lambda v: v.tensor_copy(vbias_sb[:, 512:1024], slB(3)), inc=1)
    P.wait("v", "t", t_kproj)
    P.op("v", lambda v: v.memset(v_sb[:], 1.0), inc=1)
    P.dr("v")

    v_evac = {}
    gidx = 0
    k_last_s = k_evac[TC * CH - 1][1]
    for tt in range(TT):
        for fv in range(2):
            slot = slA(gidx % 3)
            P.wait("t", "s", k_last_s)
            P.wait("t", "g:w0", g_qkvv)
            if gidx >= 3:
                P.wait("t", "v", v_evac[gidx - 3])
            mm_group(slot, [(h_sb[:, k, tt * 128:(tt + 1) * 128],
                             qkvv_sb[:, k, fv * 512:(fv + 1) * 512]) for k in range(CH)])
            tk = P.tick("t")
            P.wait("v", "t", tk)

            def ev(v, tt=tt, fv=fv, slot=slot):
                src = slot.rearrange("p (h d) -> p h d", d=64)
                dst = v_sb[:, tt, fv * 520:(fv + 1) * 520].rearrange(
                    "p (h x) -> p h x", x=65)[:, :, 0:64]
                return v.tensor_add(dst, src, vbias_sb[:, fv * 512:(fv + 1) * 512]
                                    .rearrange("p (h d) -> p h d", d=64))
            P.op("v", ev, inc=1)
            v_evac[gidx] = P.tick("v")
            gidx += 1
    P.wait("t", "s", bk_s)
    P.wait("t", "g:sm", g_esb)
    P.op("t", lambda t: t.matmul(pA[:, 1536:1568], ones_rf[:], esbrow_sb[:],
                                 start=True, stop=True), inc=1)
    t_esb = P.tick("t")
    P.wait("v", "t", t_esb)
    P.op("v", lambda v: v.tensor_copy(esb_sb[:], pA[:, 1536:1568]), inc=1)
    v_esb = P.tick("v")
    P.wait("t", "v", v_esb)

    t_p3 = P.tick("t")
    v_p3 = P.tick("v")
    s_p3 = P.tick("s")

    if n_phases <= 3:
        return _finish(nc, P, taps, locals())

    P.wait("g", "t", t_p3)
    g_outw = P.load("w1", outw_sb[:], d_outwT.ap().rearrange("(c p) f -> p c f", p=128))

    exp_tick = {}
    qk_tick = {}
    pv_tick = {}
    mult_tick = 0
    oproj_s = {}
    ln2_sv = {}
    resid_last = {}
    bcn_tick = [0]
    P.wait("t", "v", v_p3)
    P.wait("t", "s", s_p3)

    for qc in range(TC):
        if qc > 0:
            P.wait("t", "v", resid_last[qc - 1])
        for hp in range(CH):
            h0, h1 = 2 * hp, 2 * hp + 1
            LAG = 2

            def emit_qk(kt, hp=hp, qc=qc):
                seq = (qc * CH + hp) * TT + kt
                ssl = (pA[:, 0:1024], pA[:, 1024:2048], pB[:, 1024:2048])[seq % 3]
                if seq >= 3:
                    P.wait("t", "s", exp_tick.get(seq - 3, 0))
                if seq % 3 == 2 and qc > 0 and hp == 0 and kt <= 2:
                    P.wait("t", "v", ln2_sv[qc - 1][1])
                if kt == 0:
                    P.wait("t", "v", mult_tick)
                P.op("t", lambda t, hp=hp, kt=kt, qc=qc, ssl=ssl: t.matmul(
                    ssl[:, 0:512], k_sb[0:64, hp, kt * 128:(kt + 1) * 128],
                    q_sb[0:64, hp, qc * 512:(qc + 1) * 512],
                    start=True, stop=True))
                P.op("t", lambda t, hp=hp, kt=kt, qc=qc, ssl=ssl: t.matmul(
                    ssl[:, 512:1024], k_sb[64:128, hp, kt * 128:(kt + 1) * 128],
                    q_sb[64:128, hp, qc * 512:(qc + 1) * 512],
                    start=True, stop=True), inc=1)
                qk_tick[seq] = P.tick("t")
                P.wait("s", "t", qk_tick[seq])
                P.wait("s", "t", pv_tick.get(seq - 4, 0))
                P.op("s", lambda s, ssl=ssl, pslot=seq % 4: s.activation(
                    out=pt_sb[:, pslot, :], in_=ssl,
                    func=AF.Exp, scale=0.125), inc=1)
                exp_tick[seq] = P.tick("s")

            def emit_pv(kt, hp=hp, qc=qc, h0=h0, h1=h1):
                seq = (qc * CH + hp) * TT + kt
                P.wait("t", "s", exp_tick[seq])
                P.op("t", lambda t, h0=h0, kt=kt, pslot=seq % 4: t.matmul(
                    pB[0:65, 0:512], v_sb[:, kt, h0 * 65:h0 * 65 + 65],
                    pt_sb[:, pslot, 0:512], start=(kt == 0), stop=(kt == TT - 1)))
                P.op("t", lambda t, h1=h1, kt=kt, pslot=seq % 4: t.matmul(
                    pB[0:65, 512:1024], v_sb[:, kt, h1 * 65:h1 * 65 + 65],
                    pt_sb[:, pslot, 512:1024], start=(kt == 0), stop=(kt == TT - 1)),
                    inc=1 if kt % 2 == 1 else 0)
                if kt % 2 == 1:
                    tkk = P.tick("t")
                    pv_tick[seq - 1] = pv_tick[seq] = tkk

            for kt in range(TT):
                emit_qk(kt)
                if kt >= LAG:
                    emit_pv(kt - LAG)
            for kt in range(TT - LAG, TT):
                emit_pv(kt)
            P.wait("s", "t", pv_tick[(qc * CH + hp) * TT + TT - 1])
            P.op("s", lambda s: s.activation(out=rc_sb[:], in_=pB[64:65, 0:1024],
                                             func=AF.Ln), inc=1)
            P.dr("s")
            P.op("s", lambda s: s.activation(out=rcbf_sb[:], in_=rc_sb[:],
                                             func=AF.Exp, scale=-1.0), inc=1)
            t_rc = P.tick("s")
            P.wait("t", "s", t_rc)
            P.op("t", lambda t: t.matmul(pB[0:64, 1024:1536], ones_rb[0:1, 0:64],
                                         rcbf_sb[0:1, 0:512], start=True, stop=True))
            P.op("t", lambda t: t.matmul(pB[0:64, 1536:2048], ones_rb[0:1, 0:64],
                                         rcbf_sb[0:1, 512:1024], start=True, stop=True), inc=1)
            t_bcn = P.tick("t")
            P.wait("v", "t", t_bcn)
            P.op("v", lambda v: v.tensor_copy(
                bcn2_sb[0:64, :, :], pB[0:64, 1024:2048].rearrange("p (c f) -> p c f", c=2)), inc=1)
            P.dr("v")
            P.op("v", lambda v, hp=hp: v.tensor_mul(
                att_sb[0:64, hp, :], pB[0:65, 0:512][0:64, :], bcn2_sb[0:64, 0, :]))
            P.op("v", lambda v, hp=hp: v.tensor_mul(
                att_sb[64:128, hp, :], pB[0:65, 512:1024][0:64, :],
                bcn2_sb[0:64, 1, :]), inc=1)
            mult_tick = P.tick("v")

        resid_v = {}
        last_exp = exp_tick[(qc * CH + CH) * TT - 1]
        for fc in range(CH):
            oslot = pA[:, 1536:2048] if fc % 2 == 0 else pA[:, 0:512]
            P.wait("t", "v", mult_tick)
            P.wait("t", "g:w1", g_outw)
            P.wait("t", "s", last_exp)
            prev = resid_v.get(fc - 2, 0)
            P.wait("t", "v", prev)
            mm_group(oslot, [(outw_sb[:, e, fc * 128:(fc + 1) * 128],
                              att_sb[:, e, :]) for e in range(CH)])
            tk = P.tick("t")
            P.wait("v", "t", tk)
            P.op("v", lambda v, fc=fc, qc=qc, oslot=oslot: v.scalar_tensor_tensor(
                out=x_sb[:, fc, qc * 512:(qc + 1) * 512], in0=oslot,
                scalar=gg_sb[:, fc:fc + 1],
                in1=x_sb[:, fc, qc * 512:(qc + 1) * 512],
                op0=ALU.mult, op1=ALU.add), inc=1)
            resid_v[fc] = P.tick("v")
            oproj_s[(qc, fc)] = P.tick("s")

        ln2_sv[qc] = layernorm(lambda c: x_sb[:, c], lambda c: h_sb[:, c],
                               8, 24, False, [qc],
                               pre_v_tick=lambda tc, rv=resid_v[CH - 1]: rv,
                               pshome=(pB[0:1, 1024:1536], pB[0:1, 1536:2048],
                                       pB[:, 1024:1536], pB[:, 1536:2048]))
        bcn_tick[0] = P.tick("t")
        resid_last[qc] = resid_v[CH - 1]

    t_p4 = P.tick("t")
    s_p4 = P.tick("s")
    v_p4 = P.tick("v")

    if n_phases <= 4:
        return _finish(nc, P, taps, locals())

    P.wait("g", "t", t_p4)
    P.wait("g", "s", s_p4)
    P.wait("g", "v", v_p4)
    g_sgw = P.load("w0", sgw_sb[:], d_sgwT.ap().rearrange("(c p) f -> p c f", p=128))
    g_suw = P.load("w1", suw_sb[:], d_suwT.ap().rearrange("(c p) f -> p c f", p=128))
    sdw2_sb = at("sdw2_sb", [128, CH, E], BF16, R_V + 16384)
    g_sdw2 = P.load("w2", sdw2_sb[:], d_sdwT.ap().rearrange("(c p) f -> p c f", p=128))

    P.wait("t", "s", s_p4)
    P.wait("t", "v", v_p4)

    rt_evac = {}

    def emit_router_mm(tt):
        slot = pA[:, (tt % 4) * 512: (tt % 4) * 512 + NE]
        P.wait("t", "g:sm", g_rwt)
        P.wait("t", "v", v_esb)
        P.wait("t", "v", rt_evac.get(tt - 4, 0))
        mm_group(slot, [(h_sb[:, k, tt * 128:(tt + 1) * 128], rwt_sb[:, k, :])
                        for k in range(CH)])
        tk = P.tick("t")
        P.wait("v", "t", tk)
        P.op("v", lambda v, tt=tt, slot=slot: v.tensor_copy(lbuf_sb[:, tt, :], slot), inc=1)
        rt_evac[tt] = P.tick("v")

    def emit_router_math(tt):
        P.wait("s", "v", rt_evac[tt])
        P.op("s", lambda s, tt=tt: s.activation(
            out=comb_sb[:, tt, :], in_=lbuf_sb[:, tt, :], func=AF.Exp), inc=1)
        t_exp = P.tick("s")
        P.wait("v", "s", t_exp)
        P.dr("v")
        P.op("v", lambda v, tt=tt: v.reduce_sum(rr_sb[:, 0:1], comb_sb[:, tt, :],
                                                axis=mybir.AxisListType.X))
        P.dr("v")
        P.op("v", lambda v: v.reciprocal(rr_sb[:, 1:2], rr_sb[:, 0:1]))
        P.dr("v")
        P.op("v", lambda v, tt=tt: v.tensor_scalar_mul(rw_sb[:, tt, :], comb_sb[:, tt, :],
                                                       rr_sb[:, 1:2]))
        P.dr("v")
        P.op("v", lambda v, tt=tt: v.tensor_add(lbuf_sb[:, tt, :], rw_sb[:, tt, :], esb_sb[:]))
        P.dr("v")
        P.op("v", lambda v, tt=tt: v.max(out=top8_sb[:], in_=lbuf_sb[:, tt, :]))
        P.dr("v")
        P.op("v", lambda v, tt=tt: v.tensor_scalar(
            comb_sb[:, tt, :], lbuf_sb[:, tt, :], top8_sb[:, 5:6], None, op0=ALU.is_ge))
        P.dr("v")
        P.op("v", lambda v, tt=tt: v.tensor_mul(comb_sb[:, tt, :], comb_sb[:, tt, :],
                                                rw_sb[:, tt, :]))
        P.dr("v")
        P.op("v", lambda v, tt=tt: v.reduce_sum(rr_sb[:, 0:1], comb_sb[:, tt, :],
                                                axis=mybir.AxisListType.X))
        P.dr("v")
        P.op("v", lambda v: v.tensor_scalar_max(rr_sb[:, 1:2], rr_sb[:, 0:1], 1e-12))
        P.dr("v")
        P.op("v", lambda v: v.reciprocal(rr_sb[:, 0:1], rr_sb[:, 1:2]))
        P.dr("v")
        P.op("v", lambda v, tt=tt: v.tensor_scalar_mul(rw_sb[:, tt, :], comb_sb[:, tt, :],
                                                       rr_sb[:, 0:1]), inc=1)
        comb_v[tt] = P.tick("v")

    comb_v = {}
    sh_pair_v = {}
    sh_hsh_tc = {}
    gidx = 0
    for tt in range(TT):
        emit_router_mm(tt)
    for tc in range(TC):
        for ic in range(CH):
            sA_ = slA(2 * (gidx % 2))
            sB_ = slA(2 * (gidx % 2) + 1)
            P.wait("t", "g:w0", g_sgw)
            P.wait("t", "g:w1", g_suw)
            P.wait("t", "v", rt_evac[TT - 1])
            P.wait("t", "v", sh_pair_v.get(gidx - 2, 0))
            mm_group(sA_, [(sgw_sb[:, k, ic * 128:(ic + 1) * 128],
                            h_sb[:, k, tc * 512:(tc + 1) * 512]) for k in range(CH)])
            mm_group(sB_, [(suw_sb[:, k, ic * 128:(ic + 1) * 128],
                            h_sb[:, k, tc * 512:(tc + 1) * 512]) for k in range(CH)])
            tk = P.tick("t")
            P.wait("s", "t", tk)
            P.wait("s", "v", sh_pair_v.get(gidx - 2, 0))
            P.op("s", lambda s, g2=gidx % 2, sA_=sA_: s.activation(
                out=sgt_sb[:, g2, :], in_=sA_, func=AF.Silu), inc=1)
            t_silu = P.tick("s")
            if gidx < TT:
                emit_router_math(gidx)
            P.wait("v", "s", t_silu)
            P.op("v", lambda v, ic=ic, tc=tc, g2=gidx % 2, sB_=sB_: v.tensor_mul(
                hsh_sb[:, ic, tc * 512:(tc + 1) * 512], sgt_sb[:, g2, :], sB_), inc=1)
            sh_pair_v[gidx] = P.tick("v")
            gidx += 1
        sh_hsh_tc[tc] = P.tick("v")
    v_hsh = P.tick("v")
    t_sg_done = P.tick("t")

    sd_evac = {}
    gidx = 0
    for tt in range(TT):
        for hc in range(2):
            slot = slB(gidx % 4)
            P.wait("t", "g:w2", g_sdw2)
            P.wait("t", "v", sh_hsh_tc[tt // 4])
            P.wait("t", "v", sd_evac.get(gidx - 4, 0))
            mm_group(slot, [(hsh_sb[:, ic, tt * 128:(tt + 1) * 128],
                             sdw2_sb[:, ic, hc * 512:(hc + 1) * 512]) for ic in range(CH)])
            tk = P.tick("t")
            P.wait("v", "t", tk)
            P.op("v", lambda v, tt=tt, hc=hc, slot=slot: v.tensor_copy(
                y_sb[:, tt, hc * 512:(hc + 1) * 512], slot), inc=1)
            sd_evac[gidx] = P.tick("v")
            gidx += 1
    t_p5 = P.tick("t")
    v_p5 = P.tick("v")
    s_p5 = P.tick("s")

    if n_phases <= 5:
        return _finish(nc, P, taps, locals())

    gu_pair_v = {}
    y_evac_v = {}
    down_t = {}
    gu_t = {}
    for e in range(NE):
        P.wait("g", "t", gu_t.get(e - 1, t_sg_done))
        g_gup = P.load("gup", gup_sb[:], d_gupT.ap()[e].rearrange("(c p) f -> p c f", p=128))
        P.wait("g", "t", down_t.get(e - 1, t_sg_done))
        g_down = P.load("dwn", down_sb[:], d_downT.ap()[e].rearrange("(c p) f -> p c f", p=128))

        didx = [0]

        def emit_gu(tc, e=e, g_gup=g_gup):
            for i in range(4):
                gidx = tc * 4 + i
                sg_ = slA(2 * (gidx % 2))
                su_ = slA(2 * (gidx % 2) + 1)
                P.wait("t", "g:gup", g_gup)
                prev_pair = gu_pair_v.get((e, gidx - 2), gu_pair_v.get((e - 1, gidx + 14),
                                                                       v_p5))
                P.wait("t", "v", prev_pair)
                mm_group(sg_, [(gup_sb[:, k, i * 128:(i + 1) * 128],
                                h_sb[:, k, tc * 512:(tc + 1) * 512]) for k in range(CH)])
                mm_group(su_, [(gup_sb[:, k, 512 + i * 128:512 + (i + 1) * 128],
                                h_sb[:, k, tc * 512:(tc + 1) * 512]) for k in range(CH)])
                tk = P.tick("t")
                P.wait("s", "t", tk)
                P.wait("s", "v", prev_pair)
                P.op("s", lambda s, g2=gidx % 2, sg_=sg_: s.activation(
                    out=sgt_sb[:, g2, :], in_=sg_, func=AF.Silu), inc=1)
                t_silu = P.tick("s")
                P.wait("v", "s", t_silu)
                P.wait("v", "t", down_t.get((e, tc - 2), down_t.get((e - 1, tc + 2), 0)))
                P.op("v", lambda v, i=i, tcp=tc % 2, g2=gidx % 2, su_=su_: v.tensor_mul(
                    hmid_sb[:, tcp, i, :], sgt_sb[:, g2, :], su_), inc=1)
                gu_pair_v[(e, gidx)] = P.tick("v")

        def emit_down(tc, e=e, g_down=g_down):
            hm_ready = gu_pair_v[(e, tc * 4 + 3)]
            for ti in range(4):
                tt = tc * 4 + ti
                for hc in range(2):
                    slot = slB(didx[0] % 4)
                    P.wait("t", "g:dwn", g_down)
                    P.wait("t", "v", hm_ready)
                    P.wait("t", "v", y_evac_v.get((e, didx[0] - 4),
                                                  y_evac_v.get((e - 1, didx[0] + 28), v_p5)))
                    mm_group(slot, [(hmid_sb[:, tc % 2, ic, ti * 128:(ti + 1) * 128],
                                     down_sb[:, ic, hc * 512:(hc + 1) * 512])
                                    for ic in range(4)])
                    tk = P.tick("t")
                    P.wait("v", "t", tk)
                    P.wait("v", "v", comb_v[tt])
                    P.wait("v", "v", sd_evac.get(2 * tt + hc, 0))
                    P.op("v", lambda v, e=e, tt=tt, hc=hc, slot=slot: v.scalar_tensor_tensor(
                        out=y_sb[:, tt, hc * 512:(hc + 1) * 512], in0=slot,
                        scalar=rw_sb[:, tt, e:e + 1],
                        in1=y_sb[:, tt, hc * 512:(hc + 1) * 512],
                        op0=ALU.mult, op1=ALU.add), inc=1)
                    y_evac_v[(e, didx[0])] = P.tick("v")
                    didx[0] += 1
            down_t[(e, tc)] = P.tick("t")

        for tc in range(TC):
            emit_gu(tc)
            if tc == TC - 1:
                gu_t[e] = P.tick("t")
            if tc >= 1:
                emit_down(tc - 1)
        emit_down(TC - 1)
        down_t[e] = P.tick("t")
    v_p6 = P.tick("v")
    t_p6 = P.tick("t")

    if n_phases <= 6:
        return _finish(nc, P, taps, locals())

    P.wait("t", "g:sm", g_eye)
    P.wait("t", "v", v_p6)
    P.wait("s", "v", v_p6)
    y_out = 0
    tr_s = {}
    v_fill = {}
    for hc in range(CH):
        for sl in range(4):
            pslot = slB((hc * 4 + sl) % 4, 256).bitcast(BF16)
            gsl = hc * 4 + sl
            prev = tr_s.get((hc, sl - 4), tr_s.get((hc - 1, sl), 0))
            P.wait("t", "s", prev)
            for j in range(4):
                tt = sl * 4 + j
                P.op("t", lambda t, tt=tt, hc=hc, j=j, pslot=pslot: t.transpose(
                    pslot[:, j * 128:(j + 1) * 128],
                    y_sb[:, tt, hc * 128:(hc + 1) * 128], eye_sb[:]),
                    inc=1 if j == 3 else 0)
            tk = P.tick("t")
            P.wait("s", "t", tk)
            t1 = (t1a_sb, t1b_sb)[sl % 2]
            P.wait("s", "v", v_fill.get((hc, sl - 2), v_fill.get((hc - 1, sl + 2), 0)))
            P.op("s", lambda s, hc=hc, pslot=pslot, t1=t1: s.activation(
                out=t1[:], in_=pslot, func=AF.Identity,
                scale=gg_sb[:, 8 + hc:8 + hc + 1]), inc=1)
            tr_s[(hc, sl)] = P.tick("s")
            P.wait("v", "s", tr_s[(hc, sl)])
            if hc >= 2:
                P.wait("v", "g:y" if hc % 2 == 0 else "g:y2", 16 * (hc // 2))
            P.op("v", lambda v, hc=hc, sl=sl, t1=t1: v.tensor_add(
                ostage_sb[:, hc % 2, sl * 512:(sl + 1) * 512],
                x_sb[:, hc, sl * 512:(sl + 1) * 512], t1[:]), inc=1)
            v_fill[(hc, sl)] = P.tick("v")
        P.wait("y", "v", v_fill[(hc, 3)])
        P.load("y" if hc % 2 == 0 else "y2",
               d_out.ap()[hc * 128:(hc + 1) * 128, :], ostage_sb[:, hc % 2, :], eng="y")
        y_out = P.cnt["g:y"]

    return _finish(nc, P, taps, locals())


def _finish(nc, P, taps, env):
    tap_list = []
    for name in taps:
        t = env.get(name)
        if t is None:
            continue
        d = nc.dram_tensor("tap_" + name, list(t.shape), t.dtype, kind="ExternalOutput")
        tap_list.append((d, t))
    if tap_list:
        P.wait("y", "t", P.tick("t"))
        P.wait("y", "v", P.tick("v"))
        P.wait("y", "s", P.tick("s"))
        P.wait("y", "g", P.tick("g"))
        for d, t in tap_list:
            P.op("y", lambda y, d=d, t=t: y.dma_start(out=d.ap(), in_=t[:]), inc=16)
    for ch in ("g:y", "g:y2"):
        cv = P.cnt.get(ch, 0)
        if cv:
            P.ops["y"].append(lambda y, c=ch, v=cv, P=P: y.wait_ge(P.sems[c], v))
    yv = P.tick("y")
    if yv:
        P.ops["y"].append(lambda y, v=yv, P=P: y.wait_ge(P.sems["y"], v))

    from contextlib import ExitStack
    es = ExitStack()
    for e in Prog.ENGS:
        P.sems[e] = es.enter_context(nc.semaphore("sem_" + e))
    for ch in Prog.CHANNELS:
        P.sems["g:" + ch] = es.enter_context(nc.semaphore("sdma_" + ch))
    block = es.enter_context(nc.Block())
    with es:
        @block.tensor
        def _(t):
            for f in P.ops["t"]:
                f(t)

        @block.vector
        def _(v):
            for f in P.ops["v"]:
                f(v)

        @block.scalar
        def _(s):
            for f in P.ops["s"]:
                f(s)

        @block.gpsimd
        def _(g):
            for f in P.ops["g"]:
                f(g)

        @block.sync
        def _(y):
            for f in P.ops["y"]:
                f(y)
    return nc


def _stage_inputs(inputs):
    f = np.ascontiguousarray
    shared = {
        "wmod": f(inputs["L0_wmod"]),
        "qkvT": f(inputs["L0_attn_qkv_w"].T),
        "qkvN": f(inputs["L0_attn_qkv_w"][0:2 * E]),
        "rot": f(inputs["L0_attn_rotation_params"]),
        "ent": f(inputs["L0_attn_entangle_params"]),
        "outwT": f(inputs["L0_attn_out_w"].T),
        "qkv_bT": f(inputs["L0_attn_qkv_b"].reshape(24, 128).T),
        "vb_row": f(inputs["L0_attn_qkv_b"][2048:3072].reshape(1, 1024)),
        "out_bT": f(inputs["L0_attn_out_b"].reshape(8, 128).T),
        "sgwT": f(inputs["L0_ffn_shared_gate_w"].T),
        "suwT": f(inputs["L0_ffn_shared_up_w"].T),
        "sdwT": f(inputs["L0_ffn_shared_down_w"].T),
        "rwT": f(inputs["L0_ffn_router_weight"].T),
        "esb_row": f(inputs["L0_ffn_e_score_bias"].reshape(1, 32)),
        "gupT": f(inputs["L0_ffn_gate_up_proj"].transpose(0, 2, 1)),
        "downT": f(inputs["L0_ffn_down_proj"].transpose(0, 2, 1)),
    }
    x = np.asarray(inputs["L0_attn_x"])
    c = np.asarray(inputs["L0_c"])
    in_maps = []
    for b in range(8):
        m = dict(shared)
        m["xT"] = f(x[b].T)
        m["cT"] = f(c[b].reshape(8, 128).T)
        in_maps.append(m)
    return in_maps


def kernel(**inputs):
    inputs = {k: np.asarray(v, dtype=np.float32) for k, v in inputs.items()}
    in_maps = _stage_inputs(inputs)
    nc = build_program(n_phases=_N_PHASES, taps=_TAPS)
    res = run_bass_kernel_spmd(nc, in_maps, core_ids=list(range(8)))
    out = np.stack([np.ascontiguousarray(np.asarray(r["outT"]).T) for r in res.results])
    return out.astype(np.float32)


if __name__ == "__main__":
    import reference as _ref
    ins = {k: np.asarray(v) for k, v in _ref.setup_inputs().items()}
    out = kernel(**ins)
    ref = np.asarray(_ref.reference(**ins))
    err = np.mean((out - ref) ** 2) / np.mean(ref ** 2)
    print("residual variance ratio:", err)
